# version 08
# baseline (speedup 1.0000x reference)
import numpy as np
import concourse.bass as bass
import concourse.mybir as mybir
from concourse.bass_utils import run_bass_kernel_spmd

F32 = mybir.dt.float32
BF16 = mybir.dt.bfloat16
I32 = mybir.dt.int32
AF = mybir.ActivationFunctionType
ALU = mybir.AluOpType
AXX = mybir.AxisListType.X

FULL = dict(TC=512, CH=2, HT=32, IT=112, NS=16, G=2, R=128, NRANK=8, PERB=4)

V, S, T, Q, Y = "vector", "scalar", "tensor", "gpsimd", "sync"
CHAINED = (V, S, Q)


class Plan:
    ENGS = (Y, V, S, T, Q)

    def __init__(self, nc):
        self.nc = nc
        self.ops = {e: [] for e in self.ENGS}
        self.seq = {e: [nc.alloc_semaphore(name=f"seq_{e}"), 0] for e in self.ENGS}
        self.dsem = {}

    def sem(self, name):
        if name not in self.dsem:
            self.dsem[name] = [self.nc.alloc_semaphore(name=f"d_{name}"), 0]
        return self.dsem[name]

    def dtot(self, name):
        return self.dsem[name][1] if name in self.dsem else 0

    def op(self, eng, emit):
        s = self.seq[eng]
        if eng in CHAINED and s[1] > 0:
            self.ops[eng].append(lambda e, sm=s[0], v=s[1]: e.wait_ge(sm, v))
        s[1] += 1
        self.ops[eng].append(lambda e, emit=emit, sm=s[0]: emit(e).then_inc(sm, 1))
        return s[1]

    def raw(self, eng, emit):
        self.ops[eng].append(emit)

    def wait(self, eng, sem_pair, val):
        if val and val > 0:
            self.ops[eng].append(lambda e, sm=sem_pair[0], v=val: e.wait_ge(sm, v))

    def wseq(self, eng, prod, val):
        self.wait(eng, self.seq[prod], val)

    def wsem(self, eng, name, val):
        if name in self.dsem:
            self.wait(eng, self.dsem[name], val)

    def dma(self, q, emit, semname):
        s = self.sem(semname)
        s[1] += 16
        self.ops[q].append(lambda e, emit=emit, sm=s[0]: emit(e).then_inc(sm, 16))
        return s[1]

    def barrier(self):
        snap = [(self.seq[e][0], self.seq[e][1]) for e in self.ENGS]
        snap += [(v[0], v[1]) for v in self.dsem.values()]
        for e in self.ENGS:
            for sm, v in snap:
                if v > 0:
                    self.ops[e].append(lambda en, sm=sm, v=v: en.wait_ge(sm, v))

    def emit(self):
        nc = self.nc
        with nc.Block() as block:
            @block.sync
            def _(e):
                for f in self.ops[Y]:
                    f(e)

            @block.vector
            def _(e):
                for f in self.ops[V]:
                    f(e)

            @block.scalar
            def _(e):
                for f in self.ops[S]:
                    f(e)

            @block.tensor
            def _(e):
                for f in self.ops[T]:
                    f(e)

            @block.gpsimd
            def _(e):
                for f in self.ops[Q]:
                    f(e)


def build_nc(cfg):
    TC, CH, HT, IT, NS, G, R, NRANK = (cfg[k] for k in
                                       ("TC", "CH", "HT", "IT", "NS", "G", "R", "NRANK"))
    H = 128 * HT
    DT = HT
    NG = DT // G
    TT = TC // 128
    SB8 = H // 512
    DN = DT * NS
    HC = H // 512

    nc = bass.Bass("TRN2", target_bir_lowering=False)
    p = Plan(nc)

    xs = nc.dram_tensor("xs", [CH * TC, H], F32, kind="ExternalInput").ap()
    gblk = nc.dram_tensor("gblk", [IT, 128, H], I32, kind="ExternalInput").ap()
    ublk = nc.dram_tensor("ublk", [IT, 128, H], I32, kind="ExternalInput").ap()
    dblk = nc.dram_tensor("dblk", [IT, 128, H], I32, kind="ExternalInput").ap()
    gsr = nc.dram_tensor("gsr", [IT, H], F32, kind="ExternalInput").ap()
    usr = nc.dram_tensor("usr", [IT, H], F32, kind="ExternalInput").ap()
    dsr = nc.dram_tensor("dsr", [IT, H], F32, kind="ExternalInput").ap()
    xpw = nc.dram_tensor("xpw", [128, HT * 160], F32, kind="ExternalInput").ap()
    dtw = nc.dram_tensor("dtw", [R, H], F32, kind="ExternalInput").ap()
    alog = nc.dram_tensor("alog", [128, DN], F32, kind="ExternalInput").ap()
    dvec = nc.dram_tensor("dvec", [128, DT], F32, kind="ExternalInput").ap()
    dtbb = nc.dram_tensor("dtbb", [128, DT], F32, kind="ExternalInput").ap()
    eyeb = nc.dram_tensor("eyeb", [128, 128], BF16, kind="ExternalInput").ap()
    eyef = nc.dram_tensor("eyef", [128, 128], F32, kind="ExternalInput").ap()
    msk = nc.dram_tensor("msk", [1, NRANK], F32, kind="ExternalInput").ap()
    imsk = nc.dram_tensor("imsk", [1, NRANK], F32, kind="ExternalInput").ap()
    ones_r = nc.dram_tensor("ones_r", [1, 128], F32, kind="ExternalInput").ap()
    out = nc.dram_tensor("out", [CH * TC, H], F32, kind="ExternalOutput").ap()
    DBG = cfg.get("DEBUG", 0)
    if DBG:
        dbg_yg = nc.dram_tensor("dbg_yg", [128, G * TC], F32, kind="ExternalOutput").ap()
        dbg_ystg = nc.dram_tensor("dbg_ystg", [128, G * 128 * TT], F32, kind="ExternalOutput").ap()
        dbg_ndt = nc.dram_tensor("dbg_ndt", [128, G * TC], F32, kind="ExternalOutput").ap()
        dbg_bc = nc.dram_tensor("dbg_bc", [32, TC], F32, kind="ExternalOutput").ap()
        dbg_bexp = nc.dram_tensor("dbg_bexp", [128, 2 * TC], F32, kind="ExternalOutput").ap()

    wg_scr = nc.dram_tensor("wg_scr", [IT, 128, H], BF16, kind="Internal").ap()
    wu_scr = nc.dram_tensor("wu_scr", [IT, 128, H], BF16, kind="Internal").ap()
    wd_scr = nc.dram_tensor("wd_scr", [IT, 128, H], BF16, kind="Internal").ap()
    int_scr = nc.dram_tensor("int_scr", [CH, IT, 128, TC], BF16, kind="Internal").ap()
    ndt_scr = nc.dram_tensor("ndt_scr", [CH, DT, 128, TC], F32, kind="Internal").ap()
    bc_scr = nc.dram_tensor("bc_scr", [CH, 2 * NS, TC], BF16, kind="Internal").ap()
    cc_in = nc.dram_tensor("cc_in", [128, 2 * DN], F32, kind="Internal").ap()
    cc_out = nc.dram_tensor("cc_out", [NRANK * 128, 2 * DN], F32, kind="Internal").ap()
    gsr_b = nc.dram_tensor("gsr_b", [IT, H], BF16, kind="Internal").ap()
    usr_b = nc.dram_tensor("usr_b", [IT, H], BF16, kind="Internal").ap()
    dsr_b = nc.dram_tensor("dsr_b", [IT, H], BF16, kind="Internal").ap()

    def sb(name, shape, dt):
        g = nc.sbuf_tensor(name, shape, dt)
        return g.__enter__()

    xpw_b = sb("xpw_b", [128, HT * 160], BF16)
    dtw_b = sb("dtw_b", [R, H], BF16)
    apos = sb("apos", [128, DN], F32)
    dv_sb = sb("dv_sb", [128, DT], F32)
    dtb_sb = sb("dtb_sb", [128, DT], F32)
    rsumb = sb("rsumb", [128, DT], F32)
    carry = sb("carry", [128, DN], F32)
    cdtc = sb("cdtc", [128, DT], F32)
    h_in = sb("h_in", [128, DN], F32)
    mske = sb("mske", [128, NRANK], F32)
    imske = sb("imske", [128, NRANK], F32)
    eye_sb = sb("eye_sb", [128, 128], BF16)
    eyf_sb = sb("eyf_sb", [128, 128], F32)
    ones_sb = sb("ones_sb", [128, 128], BF16)
    zeros_sb = sb("zeros_sb", [128, TC], BF16)
    xnT = sb("xnT", [128, HT * TC], BF16)
    dt_inT = sb("dt_inT", [R, TC], BF16)
    ndtg = sb("ndtg", [128, G * TC], F32)
    ndtxg = sb("ndtxg", [128, G * TC], BF16)
    sp2 = sb("sp2", [128, G * TC], F32)
    yg = sb("yg", [128, G * TC], F32)
    spR = yg
    sp1 = sb("sp1", [128, G * TC], F32)
    rtmp = sb("rtmp", [128, 1], F32)
    RD = 4
    bexp = sb("bexp", [128, RD * TC], BF16)
    cexp = sb("cexp", [128, RD * TC], BF16)
    dA2 = sb("dA2", [128, 8 * TC], BF16)
    dA2r = sb("dA2r", [128, 2 * TC], F32)
    dbx = sb("dbx", [128, TC], BF16)
    hbuf = sb("hbuf", [128, TC], BF16)
    tmpb = sb("tmpb", [128, TC], BF16)
    bc32 = sb("bc32", [32, TC], F32)
    mv = sb("mv", [128, 2], F32)
    bn6 = sb("bn6", [128, 6 * SB8], F32)
    rstd = sb("rstd", [128, 1], F32)
    nmr = sb("nmr", [128, 1], F32)
    ystg = sb("ystg", [128, G * 128 * TT], F32)
    stgx = sb("stgx", [128, H], F32)
    stgx_b = stgx[:].bitcast(BF16)
    wg3 = stgx_b[:, 0:H]
    wu3 = stgx_b[:, H:2 * H]
    ccsb = stgx[:, 0:2 * DN]
    rkb = stgx[:, 2 * DN:4 * DN]
    ch1 = stgx[:, 4 * DN:5 * DN]
    ch2 = stgx[:, 5 * DN:6 * DN]
    alog_f = stgx[:, 6 * DN:7 * DN]
    xn_t = sb("xn_t", [128, H], BF16)
    wgb = sb("wgb", [128, 2 * H], BF16)
    wub = sb("wub", [128, 2 * H], BF16)
    wdb = sb("wdb", [128, H], BF16)
    sxg = sb("sxg", [128, H], BF16)
    sxd = sxg
    tanh_t = sb("tanh_t", [128, TC], BF16)
    tplus = sb("tplus", [128, TC], BF16)
    ghalf = tplus
    intT = sb("intT", [128, TC], BF16)
    intR = sb("intR", [128, 4 * TC], BF16)
    dsl = sb("dsl", [128, 4 * 1024], BF16)
    dstg = sb("dstg", [128, 1024], F32)
    eps_c = sb("eps_c", [128, 1], F32)
    one_c = sb("one_c", [128, 1], F32)

    ps = {}
    psg = {}
    porder = []

    puid = [0]
    _PSB = {F32: 512, BF16: 1024, I32: 512}

    def palloc(name, shape, dt):
        puid[0] += 1
        bank_elems = _PSB[dt]
        padded = [shape[0], ((shape[1] + bank_elems - 1) // bank_elems) * bank_elems]
        g = nc.psum_tensor(f"{name}_{puid[0]}", padded, dt)
        psg[name] = g
        h = g.__enter__()
        ps[name] = h[:, 0:shape[1]] if padded[1] != shape[1] else h
        porder.append(name)

    def pfree(*names):
        names = set(names)
        while names:
            n = porder.pop()
            assert n in names, f"psum free order: {n} not in {names}"
            names.discard(n)
            psg[n].__exit__(None, None, None)
            del psg[n], ps[n]

    last = {}

    cl = "ld_const"
    p.dma(Y, lambda e: e.dma_start(out=eye_sb[:], in_=eyeb), cl)
    p.dma(Y, lambda e: e.dma_start(out=eyf_sb[:], in_=eyef), cl)
    p.dma(Y, lambda e: e.dma_start(out=dv_sb[:], in_=dvec), cl)
    p.dma(Y, lambda e: e.dma_start(out=dtb_sb[:], in_=dtbb), cl)
    p.dma(Y, lambda e: e.dma_start(out=mske[:], in_=msk.broadcast_to([128, NRANK])), cl)
    p.dma(Y, lambda e: e.dma_start(out=imske[:], in_=imsk.broadcast_to([128, NRANK])), cl)
    p.dma(Y, lambda e: e.dma_start(out=alog_f[:], in_=alog), cl)
    gcl = "ld_gconst"
    p.dma(Q, lambda e: e.dma_start(out=xpw_b[:], in_=xpw), gcl)
    p.dma(Q, lambda e: e.dma_start(out=dtw_b[:], in_=dtw), gcl)
    n_gcl = p.dtot(gcl)
    p.dma(Q, lambda e: e.dma_start(out=gsr_b, in_=gsr), "srcast")
    p.dma(Q, lambda e: e.dma_start(out=usr_b, in_=usr), "srcast")
    p.dma(Q, lambda e: e.dma_start(out=dsr_b, in_=dsr), "srcast")
    n_srcast = p.dtot("srcast")

    p.wsem(V, cl, p.dtot(cl))
    p.op(V, lambda e: e.memset(rsumb[:], 0.0))
    p.op(V, lambda e: e.memset(h_in[:], 0.0))
    p.op(V, lambda e: e.memset(zeros_sb[:], 0.0))
    p.op(V, lambda e: e.memset(eps_c[:], 1e-5))
    p.op(V, lambda e: e.memset(ones_sb[:], 1.0))
    n_onec = p.op(V, lambda e: e.memset(one_c[:], 1.0))
    p.wsem(S, cl, p.dtot(cl))
    n_apos = p.op(S, lambda e: e.activation(apos[:], alog_f[:], AF.Exp))
    n_ones = p.seq[V][1]

    ldx, base, yadd, dadd, cadd = "ld_x", "st_base", "st_yadd", "st_dadd", "st_cadd"
    wcast, wplain, srs = "ld_wcast", "ld_wplain", "ld_srow"
    intw, intr, dslr = "st_intw", "ld_intr", "ld_dsl"
    ndtw, ndtrd, repl = "st_ndt", "ld_ndt", "ld_repl"
    bcw, scrw, ccw, ccr = "st_bc", "st_wscr", "ccw", "ccr"

    def norm_phase(ch):
        palloc("trn", [128, 128], BF16)
        palloc("trn1", [128, 128], BF16)
        if ch == 0:
            p.wseq(Y, S, n_apos)
        for tt in range(TT):
            par = 0
            stg = stgx[:, 0:H]
            w = last.get(("stg_war", par))
            if w:
                p.wait(Y, p.dsem[base + str(par)], w)
            p.dma(Y, lambda e, stg=stg, r0=ch * TC + tt * 128:
                  e.dma_start(out=stg, in_=xs[r0:r0 + 128, :]), ldx + str(par))
            nx = p.dtot(ldx + str(par))
            p.wsem(V, ldx + str(par), nx)
            for c in range(SB8):
                p.op(V, lambda e, stg=stg, c=c: e.bn_stats(
                    bn6[:, c * 6:(c + 1) * 6], stg[:, c * 512:(c + 1) * 512]))
            p.wseq(V, S, last.get(("norm_s",), 0))
            nmv = p.op(V, lambda e: e.bn_aggr(mv[:], bn6[:]))
            p.wseq(S, V, nmv)
            p.wseq(S, V, n_onec)
            p.op(S, lambda e: e.activation(rstd[:], mv[:, 1:2], AF.Ln, bias=eps_c[:]))
            nr = p.op(S, lambda e: e.activation(rstd[:], rstd[:], AF.Exp, scale=-0.5))
            p.wseq(V, S, nr)
            nn = p.op(V, lambda e: e.scalar_tensor_tensor(
                nmr[:], mv[:, 0:1], -1.0, rstd[:], ALU.mult, ALU.mult))
            p.wseq(V, T, last.get(("xn_T_war",), 0))
            w = last.get((("sxu_war", 0),))
            if w:
                p.wseq(V, w[0], w[1])
            nxn = p.op(V, lambda e, stg=stg: e.tensor_scalar(
                xn_t[:], stg, rstd[:], nmr[:], ALU.mult, ALU.add))
            last[("norm_s",)] = p.seq[S][1]
            p.wseq(Y, V, nxn)
            p.dma(Y, lambda e, stg=stg, r0=ch * TC + tt * 128:
                  e.dma_start(out=out[r0:r0 + 128, :], in_=stg), base + str(par))
            last[("stg_war", par)] = p.dtot(base + str(par))
            for ht in range(HT):
                pnm = "trn" if ht % 2 == 0 else "trn1"
                p.wseq(T, V, nxn)
                p.wseq(T, S, last.get(("trn_ev", pnm), 0))
                ntr = p.op(T, lambda e, ht=ht, ph=ps[pnm]: e.transpose(
                    ph[:], xn_t[:, ht * 128:(ht + 1) * 128], eye_sb[:]))
                p.wseq(S, T, ntr)
                nev = p.op(S, lambda e, ht=ht, tt=tt, ph=ps[pnm]: e.copy(
                    xnT[:, ht * TC + tt * 128: ht * TC + tt * 128 + 128],
                    ph[:]))
                last[("trn_ev", pnm)] = nev
            last[("xn_T_war",)] = p.seq[T][1]
        last[("xnT", ch)] = p.seq[S][1]
        pfree("trn", "trn1")

    def xp_phase(ch):
        palloc("xp", [128, TC], F32)
        palloc("xp2", [32, TC], F32)
        nxnt = last[("xnT", ch)]
        p.wseq(T, S, nxnt)
        p.wsem(T, gcl, n_gcl)
        nxp = None
        for ht in range(HT):
            st, spp = (ht == 0), (ht == HT - 1)
            p.raw(T, lambda e, ht=ht, st=st, spp=spp, ph=ps["xp"]: e.matmul(
                ph[:], xpw_b[:, ht * 160: ht * 160 + 128],
                xnT[:, ht * TC:(ht + 1) * TC], start=st, stop=spp))
            em = lambda e, ht=ht, st=st, spp=spp, ph=ps["xp2"]: e.matmul(
                ph[:], xpw_b[:, ht * 160 + 128: ht * 160 + 160],
                xnT[:, ht * TC:(ht + 1) * TC], start=st, stop=spp)
            if spp:
                nxp = p.op(T, em)
            else:
                p.raw(T, em)
        p.wseq(S, T, nxp)
        ndti = p.op(S, lambda e, ph=ps["xp"]: e.copy(dt_inT[:], ph[:]))
        last[("dt_inT", ch)] = ndti
        p.wseq(V, T, nxp)
        nbc = p.op(V, lambda e, ph=ps["xp2"]: e.tensor_copy(bc32[:], ph[:]))
        p.wseq(Q, V, nbc)
        p.dma(Q, lambda e, ch=ch: e.dma_start(out=bc_scr[ch], in_=bc32[:]), bcw)
        pfree("xp", "xp2")
        p.wseq(T, S, ndti)
        p.wseq(T, V, nbc)

    def dt_group(ch, g):
        SUB = cfg.get("AMSUB", 9)
        ndti = last[("dt_inT", ch)]
        s_entry = p.seq[S][1]
        t_entry = last.get(("ytr_T",), 0)
        nstt = []
        for k in range(G):
            d = g * G + k
            pnm = "dt" if k % 2 == 0 else "dt1"
            p.wseq(T, S, last.get(("psdt_s", pnm), 0))
            p.wseq(T, V, last.get(("psdt_v", pnm), 0))
            p.wseq(T, S, ndti)
            p.wsem(T, gcl, n_gcl)
            nmm = p.op(T, lambda e, d=d, ph=ps[pnm]: e.matmul(
                ph[:], dtw_b[:, d * 128:(d + 1) * 128], dt_inT[:],
                start=True, stop=True))
            p.wseq(S, T, nmm)
            p.wseq(S, V, last.get(("sp1_war",), 0))
            na = p.op(S, lambda e, k=k, d=d, ph=ps[pnm]: e.activation(
                sp1[:, k * TC:(k + 1) * TC], ph[:], AF.Abs,
                bias=dtb_sb[:, d:d + 1]))
            p.wseq(V, T, nmm)
            p.wseq(V, S, na)
            p.wseq(V, T, t_entry)
            nrl = p.op(V, lambda e, k=k, d=d, ph=ps[pnm]: e.tensor_scalar(
                spR[:, k * TC:(k + 1) * TC], ph[:], dtb_sb[:, d:d + 1], 0.0,
                ALU.add, ALU.max))
            last[("psdt_s", pnm)] = na
            last[("psdt_v", pnm)] = nrl
        if SUB < 2:
            last[("sp1_war",)] = p.seq[V][1]
            last[("ndtg", ch, g)] = p.seq[V][1]
            return
        for k in range(G):
            p.op(S, lambda e, k=k: e.activation(
                sp2[:, k * TC:(k + 1) * TC], sp1[:, k * TC:(k + 1) * TC],
                AF.Exp, scale=-1.0))
        for k in range(G):
            nl = p.op(S, lambda e, k=k: e.activation(
                sp1[:, k * TC:(k + 1) * TC], sp2[:, k * TC:(k + 1) * TC],
                AF.Ln, bias=one_c[:]))
        if SUB < 3:
            last[("sp1_war",)] = p.seq[V][1]
            last[("ndtg", ch, g)] = p.seq[V][1]
            return
        for k in range(G):
            d = g * G + k
            p.wseq(V, S, nl)
            p.wseq(V, S, s_entry)
            p.wsem(V, ndtw, last.get(("ndtw_war",), 0))
            n1 = p.op(V, lambda e, k=k: e.scalar_tensor_tensor(
                ndtg[:, k * TC:(k + 1) * TC], sp1[:, k * TC:(k + 1) * TC], -1.0,
                spR[:, k * TC:(k + 1) * TC], ALU.mult, ALU.subtract))
            nstt.append(n1)
            p.op(V, lambda e, k=k, d=d: e.tensor_mul(
                ndtxg[:, k * TC:(k + 1) * TC], ndtg[:, k * TC:(k + 1) * TC],
                xnT[:, d * TC:(d + 1) * TC]))
            p.wseq(V, T, t_entry)
            p.op(V, lambda e, k=k, d=d: e.tensor_scalar(
                yg[:, k * TC:(k + 1) * TC], xnT[:, d * TC:(d + 1) * TC],
                dv_sb[:, d:d + 1], None, ALU.mult))
            p.op(V, lambda e, k=k: e.reduce_sum(
                rtmp[:], ndtg[:, k * TC:(k + 1) * TC], AXX))
            p.op(V, lambda e, d=d: e.tensor_add(
                rsumb[:, d:d + 1], rsumb[:, d:d + 1], rtmp[:]))
            if SUB >= 4:
                p.wseq(Y, V, n1)
                p.dma(Y, lambda e, ch=ch, d=d, k=k: e.dma_start(
                    out=ndt_scr[ch, d], in_=ndtg[:, k * TC:(k + 1) * TC]), ndtw)
        last[("sp1_war",)] = nstt[-1]
        last[("ndtw_war",)] = p.dtot(ndtw)
        last[("ndtg", ch, g)] = p.seq[V][1]

    def scan_slice(ch, g, n, slice_idx, n_slices):
        par = slice_idx % RD
        rtarget = 32 * (slice_idx // RD + 1) + last[("repl0", ch, par)]
        p.wsem(V, repl + str(par), rtarget)
        p.wsem(S, repl + str(par), rtarget)
        for k in range(G):
            d = g * G + k
            apar = (slice_idx * G + k) % 8
            p.wseq(S, V, last[("ndtg", ch, g)])
            p.wseq(S, V, last.get(("dA_used", apar), 0))
            nda = p.op(S, lambda e, k=k, d=d, n=n, apar=apar: e.activation(
                dA2[:, apar * TC:(apar + 1) * TC], ndtg[:, k * TC:(k + 1) * TC],
                AF.Exp, scale=apos[:, d * NS + n: d * NS + n + 1]))
            p.op(V, lambda e, k=k, par=par: e.scalar_tensor_tensor(
                dbx[:], bexp[:, par * TC:(par + 1) * TC], -1.0,
                ndtxg[:, k * TC:(k + 1) * TC], ALU.mult, ALU.mult))
            p.wseq(V, S, nda)
            init = 0.0 if ch == 0 else carry[:, d * NS + n: d * NS + n + 1]
            nh = p.op(V, lambda e, apar=apar, init=init: e.tensor_tensor_scan(
                hbuf[:], dA2[:, apar * TC:(apar + 1) * TC], dbx[:], init,
                ALU.mult, ALU.add))
            last[("dA_used", apar)] = nh
            p.op(V, lambda e, d=d, n=n: e.tensor_copy(
                carry[:, d * NS + n: d * NS + n + 1], hbuf[:, TC - 1:TC]))
            p.op(V, lambda e, par=par: e.tensor_mul(
                tmpb[:], hbuf[:], cexp[:, par * TC:(par + 1) * TC]))
            p.op(V, lambda e, k=k: e.tensor_add(
                yg[:, k * TC:(k + 1) * TC], yg[:, k * TC:(k + 1) * TC], tmpb[:]))
        last[("slice_v", par)] = p.seq[V][1]
        nxt = slice_idx + RD - 1
        if nxt < n_slices:
            npar = nxt % RD
            nn2 = nxt % NS
            p.wseq(Y, V, last.get(("slice_v", npar), 0))
            p.dma(Y, lambda e, ch=ch, n=nn2, npar=npar: e.dma_start(
                out=bexp[:, npar * TC:(npar + 1) * TC],
                in_=bc_scr[ch, n:n + 1, :].broadcast_to([128, TC])), repl + str(npar))
            p.dma(Y, lambda e, ch=ch, n=nn2, npar=npar: e.dma_start(
                out=cexp[:, npar * TC:(npar + 1) * TC],
                in_=bc_scr[ch, NS + n: NS + n + 1, :].broadcast_to([128, TC])),
                repl + str(npar))

    def y_flush(ch, g, which, trps):
        if DBG and ch == 0 and g == 0 and which == yadd:
            nv = p.seq[V][1]
            p.wseq(Y, V, nv)
            p.dma(Y, lambda e: e.dma_start(out=dbg_yg, in_=yg[:]), "dbgy")
            p.dma(Y, lambda e: e.dma_start(out=dbg_ndt, in_=ndtg[:]), "dbgy")
            p.dma(Y, lambda e: e.dma_start(out=dbg_bc, in_=bc32[:]), "dbgy")
            p.dma(Y, lambda e: e.dma_start(out=dbg_bexp, in_=bexp[:]), "dbgy")
        """Transpose yg (f32) and DMA-accumulate into out at cols g*G*128.."""
        nyg = p.seq[V][1]
        gate = last.get(("flush_gate", which, ch))
        for k in range(G):
            for tt in range(TT):
                p.wseq(T, V, nyg)
                p.wseq(T, V, last.get(("ytr_used",), 0))
                ntr = p.op(T, lambda e, k=k, tt=tt, ph=ps[trps]: e.transpose(
                    ph[:], yg[:, k * TC + tt * 128: k * TC + tt * 128 + 128],
                    eyf_sb[:]))
                p.wseq(V, T, ntr)
                w = last.get(("ystg_used",))
                if w:
                    p.wait(V, p.dsem[w[0]], w[1])
                nev = p.op(V, lambda e, k=k, tt=tt, ph=ps[trps]: e.tensor_copy(
                    ystg[:, tt * (G * 128) + k * 128: tt * (G * 128) + (k + 1) * 128],
                    ph[:]))
                last[("ytr_used",)] = nev
        last[("ytr_T",)] = p.seq[T][1]
        nys = p.seq[V][1]
        for tt in range(TT):
            p.wseq(Q, V, nys)
            p.wsem(Q, base + "0", p.dtot(base + "0"))
            p.wsem(Q, base + "1", p.dtot(base + "1"))
            if gate:
                p.wsem(Q, gate[0], gate[1])
                p.wsem(Q, gate[2], gate[3])
            p.dma(Q, lambda e, ch=ch, g=g, tt=tt: e.dma_start(
                out=out[ch * TC + tt * 128: ch * TC + (tt + 1) * 128,
                        g * G * 128:(g + 1) * G * 128],
                in_=ystg[:, tt * G * 128:(tt + 1) * G * 128],
                accum_op=ALU.add), which)
        last[("ystg_used",)] = (which, p.dtot(which))
        if DBG and ch == 0 and g == 0 and which == yadd:
            p.wseq(Y, V, nys)
            p.dma(Y, lambda e: e.dma_start(out=dbg_ystg, in_=ystg[:]), "dbgy")

    def sexp_repl(j, table, dst_sb, warkey, semname):
        ww = last.get((warkey,))
        if ww:
            p.wseq(Y, ww[0], ww[1])
        p.wsem(Y, "srcast", n_srcast)
        p.dma(Y, lambda e, j=j, table=table, dst=dst_sb: e.dma_start(
            out=dst, in_=table[j:j + 1, :].broadcast_to([128, H])), semname)
        return p.dtot(semname)

    def wbufs(j):
        par = j % 3
        if par == 2:
            return par, wg3, wu3
        return par, wgb[:, par * H:(par + 1) * H], wub[:, par * H:(par + 1) * H]

    def ffn_pre(ch, j):
        par, wg, wu = wbufs(j)
        ldq = Q if ch == 0 else Y
        p.wait(ldq, p.seq[T], last.get(("wbuf_used", par), 0))
        p.wsem(ldq, scrw + str(par), last.get(("wscr_war", par), 0))
        if par == 2:
            p.wsem(ldq, base + "0", p.dtot(base + "0"))
        if ch == 0:
            p.wseq(ldq, V, last.get(("wdq", par), 0))
            wsem = wcast + str(par)
            p.dma(Q, lambda e, j=j, wg=wg: e.dma_start(out=wg, in_=gblk[j]), wsem)
            p.dma(Q, lambda e, j=j, wu=wu: e.dma_start(out=wu, in_=ublk[j]), wsem)
        else:
            wsem = wplain + str(par)
            p.dma(Y, lambda e, j=j, wg=wg: e.dma_start(out=wg, in_=wg_scr[j]), wsem)
            p.dma(Y, lambda e, j=j, wu=wu: e.dma_start(out=wu, in_=wu_scr[j]), wsem)
        wtot = p.dtot(wsem)
        ndq = 0
        if ch == 0:
            sp2_ = j % 2
            gslot = sxg[:] if sp2_ == 0 else dsl[:, 0:H]
            uslot = xn_t[:] if sp2_ == 0 else wdb[:]
            nse_g = sexp_repl(j, gsr_b, gslot, ("sxg_war", sp2_), "sxrepg" + str(sp2_))
            nse_u = sexp_repl(j, usr_b, uslot, ("sxu_war", sp2_), "sxrepu" + str(sp2_))
            p.wsem(V, wsem, wtot)
            p.wsem(V, "sxrepg" + str(sp2_), nse_g)
            ndq_g = p.op(V, lambda e, wg=wg, gslot=gslot: e.tensor_mul(wg, wg, gslot))
            last[(("sxg_war", sp2_),)] = (V, ndq_g)
            p.wsem(V, "sxrepu" + str(sp2_), nse_u)
            ndq_u = p.op(V, lambda e, wu=wu, uslot=uslot: e.tensor_mul(wu, wu, uslot))
            last[(("sxu_war", sp2_),)] = (V, ndq_u)
            last[("wdq", par)] = ndq_u
            p.wseq(Y, V, ndq_g)
            p.dma(Y, lambda e, j=j, wg=wg: e.dma_start(out=wg_scr[j], in_=wg),
                  scrw + str(par))
            p.wseq(Y, V, ndq_u)
            p.dma(Y, lambda e, j=j, wu=wu: e.dma_start(out=wu_scr[j], in_=wu),
                  scrw + str(par))
            last[("wscr_war", par)] = p.dtot(scrw + str(par))
            ndq = ndq_u
        last[("ffn_pre", j)] = (wsem, wtot, ndq)

    def ffn_mm(ch, j):
        par, wg, wu = wbufs(j)
        wsem, wtot, ndq = last.pop(("ffn_pre", j))
        p.wseq(T, S, last[("xnT", ch)])
        if ch == 0:
            p.wseq(T, V, ndq)
        else:
            p.wsem(T, wsem, wtot)
        p.wseq(T, S, last.get(("psg_s",), 0))
        p.wseq(T, V, last.get(("psg_v",), 0))
        ng = nu = None
        for ht in range(HT):
            em = lambda e, ht=ht, wg=wg, ph=ps["g"]: e.matmul(
                ph[:], wg[:, ht * 128:(ht + 1) * 128],
                xnT[:, ht * TC:(ht + 1) * TC], start=(ht == 0), stop=(ht == HT - 1))
            ng = p.op(T, em) if ht == HT - 1 else p.raw(T, em)
        p.wseq(T, V, last.get(("psu_v",), 0))
        for ht in range(HT):
            em = lambda e, ht=ht, wu=wu, ph=ps["u"]: e.matmul(
                ph[:], wu[:, ht * 128:(ht + 1) * 128],
                xnT[:, ht * TC:(ht + 1) * TC], start=(ht == 0), stop=(ht == HT - 1))
            nu = p.op(T, em) if ht == HT - 1 else p.raw(T, em)
        last[("wbuf_used", par)] = p.seq[T][1]
        p.wseq(S, T, ng)
        p.wseq(S, V, last.get(("tanh_war",), 0))
        nth = p.op(S, lambda e, ph=ps["g"]: e.activation(tanh_t[:], ph[:], AF.Tanh, scale=0.5))
        p.wseq(V, S, nth)
        ntp = p.op(V, lambda e: e.tensor_scalar(
            tplus[:], tanh_t[:], 0.5, 0.5, ALU.mult, ALU.add))
        last[("tanh_war",)] = ntp
        p.wseq(V, T, ng)
        ngh = p.op(V, lambda e, ph=ps["g"]: e.tensor_mul(ghalf[:], ph[:], tplus[:]))
        last[("psg_s",)] = nth
        last[("psg_v",)] = ngh
        p.wseq(V, T, nu)
        p.wsem(V, intw + "0", last.get(("intw_war", 0), 0))
        nit = p.op(V, lambda e, ph=ps["u"]: e.tensor_mul(
            intT[:, 0:TC], ph[:], ghalf[:]))
        last[("psu_v",)] = nit
        p.wseq(Y, V, nit)
        p.dma(Y, lambda e, ch=ch, j=j: e.dma_start(
            out=int_scr[ch, j], in_=intT[:, 0:TC]), intw + "0")
        last[("intw_war", 0)] = p.dtot(intw + "0")

    def down_itile(j):
        p.wait(Q, p.seq[Q], last.get(("wdb_used",), 0))
        p.wsem(Q, "wdscr", last.get(("wdb_war",), 0))
        p.dma(Q, lambda e, j=j: e.dma_start(out=wdb[:], in_=dblk[j]), "wdcast")
        wtot = p.dtot("wdcast")
        nse = sexp_repl(j, dsr_b, sxd[:], "sxd_war", "sxrepd")
        p.wsem(Q, "wdcast", wtot)
        p.wsem(Q, "sxrepd", nse)
        ndq = p.op(Q, lambda e: e.tensor_mul(wdb[:], wdb[:], sxd[:]))
        last[("wdb_used",)] = ndq
        last[("sxd_war",)] = (Q, ndq)
        p.wseq(Y, Q, ndq)
        p.dma(Y, lambda e, j=j: e.dma_start(out=wd_scr[j], in_=wdb[:]), "wdscr")
        last[("wdb_war",)] = p.dtot("wdscr")

    def a_phase(ch):
        AM = cfg.get("AMODE", 9)
        norm_phase(ch)
        if AM < 2:
            return
        xp_phase(ch)
        if AM < 3:
            return
        palloc("g", [128, TC], F32)
        palloc("u", [128, TC], F32)
        palloc("dt", [128, TC], F32)
        palloc("dt1", [128, TC], F32)
        palloc("try", [128, 128], F32)
        for s0 in range(RD):
            last[("repl0", ch, s0)] = p.dtot(repl + str(s0))
        if AM >= 4:
            p.wsem(Y, bcw, p.dtot(bcw))
            for s0 in range(RD - 1):
                nn0 = s0 % NS
                p.wseq(Y, V, last.get(("slice_v", s0), 0))
                p.dma(Y, lambda e, ch=ch, nn0=nn0, s0=s0: e.dma_start(
                    out=bexp[:, s0 * TC:(s0 + 1) * TC],
                    in_=bc_scr[ch, nn0:nn0 + 1, :].broadcast_to([128, TC])),
                    repl + str(s0))
                p.dma(Y, lambda e, ch=ch, nn0=nn0, s0=s0: e.dma_start(
                    out=cexp[:, s0 * TC:(s0 + 1) * TC],
                    in_=bc_scr[ch, NS + nn0: NS + nn0 + 1,
                               :].broadcast_to([128, TC])),
                    repl + str(s0))
        units = [("ffn", j) for j in range(IT)]
        if ch == 1:
            units = [u for j in range(IT) for u in (("ffn", j), ("down", j))]
        ffn_pre(ch, 0)
        sc = []
        if AM >= 4:
            for g in range(NG):
                sc.append(("dtg", g))
                if AM >= 5:
                    for n in range(NS):
                        sc.append(("scan", g, n))
                if AM >= 6:
                    sc.append(("yfl", g))
        merged = []
        si = 0
        for i, u in enumerate(units):
            merged.append(u)
            want = (i + 1) * len(sc) // len(units)
            while si < want:
                merged.append(sc[si])
                si += 1
        merged.extend(sc[si:])
        slice_idx = 0
        for u in merged:
            if u[0] == "ffn":
                if u[1] + 1 < IT:
                    ffn_pre(ch, u[1] + 1)
                ffn_mm(ch, u[1])
            elif u[0] == "down":
                down_itile(u[1])
            elif u[0] == "dtg":
                dt_group(ch, u[1])
            elif u[0] == "yfl":
                y_flush(ch, u[1], yadd, "try")
            else:
                scan_slice(ch, u[1], u[2], slice_idx, NG * NS)
                slice_idx += 1
        pfree("g", "u", "dt", "dt1", "try")

    def ag_chain():
        nr = p.seq[V][1]
        for d in range(DT):
            p.wseq(S, V, nr)
            p.op(S, lambda e, d=d: e.activation(
                ccsb[:, DN + d * NS: DN + (d + 1) * NS],
                apos[:, d * NS:(d + 1) * NS], AF.Exp, scale=rsumb[:, d:d + 1]))
        nee = p.seq[S][1]
        ncp = p.op(V, lambda e: e.tensor_copy(ccsb[:, 0:DN], carry[:]))
        p.wseq(Y, V, ncp)
        p.wseq(Y, S, nee)
        p.dma(Y, lambda e: e.dma_start(out=cc_in, in_=ccsb[:]), ccw)
        ccs = p.sem("ccdone")
        p.wsem(Q, ccw, p.dtot(ccw))
        ccs[1] += 1
        p.ops[Q].append(lambda e, sm=ccs[0]: e.collective_compute(
            "AllGather", ALU.bypass, replica_groups=[list(range(NRANK))],
            ins=[cc_in], outs=[cc_out]).then_inc(sm))
        p.wait(Y, ccs, 1)
        prev = 0
        for r in range(NRANK):
            p.wseq(Y, V, prev)
            p.dma(Y, lambda e, r=r: e.dma_start(
                out=rkb[:], in_=cc_out[r * 128:(r + 1) * 128, :]), ccr)
            p.wsem(V, ccr, p.dtot(ccr))
            p.op(V, lambda e, r=r: e.tensor_scalar(
                ch1[:], rkb[:, DN:2 * DN], mske[:, r:r + 1], imske[:, r:r + 1],
                ALU.mult, ALU.add))
            p.op(V, lambda e: e.tensor_mul(ch1[:], h_in[:], ch1[:]))
            p.op(V, lambda e, r=r: e.tensor_scalar(
                ch2[:], rkb[:, 0:DN], mske[:, r:r + 1], None, ALU.mult))
            prev = p.op(V, lambda e: e.tensor_add(h_in[:], ch1[:], ch2[:]))
        last[("chain",)] = prev

    def corr_group(ch, g, slice_base):
        ncd = 0
        for k in range(G):
            d = g * G + k
            par = k % 2
            p.wseq(Y, V, last.get(("ndtr_used", par), 0))
            p.dma(Y, lambda e, ch=ch, d=d, par=par: e.dma_start(
                out=dA2r[:, par * TC:(par + 1) * TC], in_=ndt_scr[ch, d]),
                ndtrd + str(par))
            p.wsem(V, ndtrd + str(par), p.dtot(ndtrd + str(par)))
            init = 0.0 if ch == 0 else cdtc[:, d:d + 1]
            p.wseq(V, S, last.get(("corr_s",), 0))
            ncs = p.op(V, lambda e, k=k, par=par, init=init: e.tensor_tensor_scan(
                sp1[:, k * TC:(k + 1) * TC], dA2r[:, par * TC:(par + 1) * TC],
                zeros_sb[:], init, ALU.add, ALU.add))
            last[("ndtr_used", par)] = ncs
            p.op(V, lambda e, k=k, d=d: e.tensor_copy(
                cdtc[:, d:d + 1], sp1[:, k * TC + TC - 1: k * TC + TC]))
            ncd = p.seq[V][1]
        for n in range(NS):
            slice_idx = slice_base + n
            par = slice_idx % 2
            cpar = slice_idx % RD
            rtarget = 16 * (slice_idx // RD + 1) + last[("crepl0", ch, cpar)]
            p.wsem(V, repl + "c" + str(cpar), rtarget)
            p.wsem(S, repl + "c" + str(cpar), rtarget)
            for k in range(G):
                d = g * G + k
                apar = (slice_idx * G + k) % 4
                p.wseq(S, V, ncd)
                p.wseq(S, V, last.get(("ec_used", apar), 0))
                nec = p.op(S, lambda e, k=k, d=d, n=n, apar=apar: e.activation(
                    bexp[:, apar * TC:(apar + 1) * TC], sp1[:, k * TC:(k + 1) * TC],
                    AF.Exp, scale=apos[:, d * NS + n: d * NS + n + 1]))
                last[("corr_s",)] = nec
                p.wseq(V, S, nec)
                p.wseq(V, V, last[("chain",)])
                ntm = p.op(V, lambda e, d=d, n=n, apar=apar, cpar=cpar:
                           e.scalar_tensor_tensor(
                    tmpb[:], bexp[:, apar * TC:(apar + 1) * TC],
                    h_in[:, d * NS + n: d * NS + n + 1],
                    cexp[:, cpar * TC:(cpar + 1) * TC], ALU.mult, ALU.mult))
                last[("ec_used", apar)] = ntm
                if n == 0:
                    p.wseq(V, T, last.get(("ytr_T",), 0))
                    p.op(V, lambda e, k=k: e.tensor_copy(
                        yg[:, k * TC:(k + 1) * TC], tmpb[:]))
                else:
                    p.op(V, lambda e, k=k: e.tensor_add(
                        yg[:, k * TC:(k + 1) * TC], yg[:, k * TC:(k + 1) * TC],
                        tmpb[:]))
            nxt = slice_idx + RD - 1
            if nxt < NG * NS:
                npar = nxt % RD
                nn2 = nxt % NS
                p.wseq(Y, V, last.get(("cslice_v", npar), 0))
                p.dma(Y, lambda e, ch=ch, nn2=nn2, npar=npar: e.dma_start(
                    out=cexp[:, npar * TC:(npar + 1) * TC],
                    in_=bc_scr[ch, NS + nn2: NS + nn2 + 1,
                               :].broadcast_to([128, TC])), repl + "c" + str(npar))
            last[("cslice_v", cpar)] = p.seq[V][1]

    BW = 1024 if H >= 1024 else 512

    def b_hc_unit(ch, hp):
        for s in range(TT):
            palloc(f"pd{s}", [128, BW], F32)
        nmm = None
        for j in range(IT):
            par = j % 4
            p.wait(Y, p.seq[T], last.get(("intR_used", par), 0))
            p.dma(Y, lambda e, ch=ch, j=j, par=par: e.dma_start(
                out=intR[:, par * TC:(par + 1) * TC], in_=int_scr[ch, j]),
                intr + str(par))
            p.wait(S, p.seq[T], last.get(("intR_used", par), 0))
            p.dma(S, lambda e, j=j, hp=hp, par=par: e.dma_start(
                out=dsl[:, par * BW:(par + 1) * BW],
                in_=wd_scr[j, :, hp * BW:(hp + 1) * BW]), dslr + str(par))
            p.wsem(T, intr + str(par), p.dtot(intr + str(par)))
            p.wsem(T, dslr + str(par), p.dtot(dslr + str(par)))
            if j == 0:
                for s in range(TT):
                    w = last.get(("pd_used", s))
                    if w:
                        p.wseq(T, V, w)
                p.wseq(T, V, last.get(("ytr_used",), 0))
            for s in range(TT):
                for hh in range(BW // 512):
                    em = lambda e, par=par, s=s, j=j, hh=hh, ph=ps[f"pd{s}"]: e.matmul(
                            ph[:, hh * 512:(hh + 1) * 512],
                            intR[:, par * TC + s * 128: par * TC + (s + 1) * 128],
                            dsl[:, par * BW + hh * 512: par * BW + (hh + 1) * 512],
                            start=(j == 0), stop=(j == IT - 1))
                    if s == TT - 1 and hh == BW // 512 - 1:
                        nmm = p.op(T, em)
                    else:
                        p.raw(T, em)
            last[("intR_used", par)] = p.seq[T][1]
        for s in range(TT):
            p.wseq(V, T, nmm)
            p.wsem(V, dadd, last.get(("dstg_war", 0), 0))
            nev = p.op(V, lambda e, s=s, ph=ps[f"pd{s}"]: e.tensor_copy(
                dstg[:, 0:BW], ph[:]))
            last[("pd_used", s)] = nev
            p.wseq(Q, V, nev)
            p.dma(Q, lambda e, ch=ch, hp=hp, s=s: e.dma_start(
                out=out[ch * TC + s * 128: ch * TC + (s + 1) * 128,
                        hp * BW:(hp + 1) * BW],
                in_=dstg[:, 0:BW], accum_op=ALU.add), dadd)
            last[("dstg_war", 0)] = p.dtot(dadd)
        pfree(*[f"pd{s}" for s in range(TT)])

    def b_and_corr(ch):
        for s0 in range(RD):
            last[("crepl0", ch, s0)] = p.dtot(repl + "c" + str(s0))
        for s0 in range(RD - 1):
            nn0 = s0 % NS
            p.wseq(Y, V, last.get(("cslice_v", s0), 0))
            p.dma(Y, lambda e, ch=ch, nn0=nn0, s0=s0: e.dma_start(
                out=cexp[:, s0 * TC:(s0 + 1) * TC],
                in_=bc_scr[ch, NS + nn0: NS + nn0 + 1,
                           :].broadcast_to([128, TC])), repl + "c" + str(s0))
        HP = H // BW
        bi = 0
        for g in range(NG):
            need_hp = ((g + 1) * G * 128 + BW - 1) // BW
            while bi < min(need_hp, HP):
                b_hc_unit(ch, bi)
                bi += 1
            corr_group(ch, g, g * NS)
            last[("flush_gate", cadd, ch)] = (dadd, p.dtot(dadd),
                                              dadd, p.dtot(dadd))
            palloc("tryc", [128, 128], F32)
            for s in range(TT):
                w = last.get(("pd_used", s))
                if w:
                    p.wseq(T, V, w)
            y_flush(ch, g, cadd, "tryc")
            pfree("tryc")
        while bi < HP:
            b_hc_unit(ch, bi)
            bi += 1

    PH = cfg.get("PHASES", 5)
    a_phase(0)
    p.barrier()
    if PH >= 2:
        a_phase(1)
        p.barrier()
    if PH >= 3:
        ag_chain()
    if PH >= 4:
        b_and_corr(0)
        p.barrier()
    if PH >= 5:
        b_and_corr(1)

    for nm, v in p.dsem.items():
        p.wait(Y, v, v[1])
    for e in Plan.ENGS:
        if e != Y:
            p.wait(Y, p.seq[e], p.seq[e][1])

    p.emit()
    return nc



def _prep_shared(A_log, Dv, x_proj_w, dt_proj_w, dt_proj_b,
                 gate_q, gate_s, up_q, up_s, down_q, down_s, cfg):
    HT, IT, NS = cfg["HT"], cfg["IT"], cfg["NS"]
    H = 128 * HT
    DT = HT
    c = np.ascontiguousarray

    def gu_blocks(q):
        return c(np.asarray(q, np.int32).T.reshape(HT, 128, IT, 128)
                 .transpose(2, 1, 0, 3).reshape(IT, 128, H))

    def gu_srows(s):
        return c(np.asarray(s, np.float32).reshape(IT, 128, HT)
                 .transpose(0, 2, 1).reshape(IT, H))

    import ml_dtypes
    return dict(
        gblk=gu_blocks(gate_q), ublk=gu_blocks(up_q),
        dblk=c(np.asarray(down_q, np.int32).T.reshape(IT, 128, H)),
        gsr=gu_srows(gate_s), usr=gu_srows(up_s),
        dsr=c(np.asarray(down_s, np.float32).T),
        xpw=c(np.asarray(x_proj_w, np.float32).T.reshape(HT, 128, 160)
              .transpose(1, 0, 2).reshape(128, HT * 160)),
        dtw=c(np.asarray(dt_proj_w, np.float32).T),
        alog=c(np.asarray(A_log, np.float32).reshape(DT, 128, NS)
               .transpose(1, 0, 2).reshape(128, DT * NS)),
        dvec=c(np.asarray(Dv, np.float32).reshape(DT, 128).T),
        dtbb=c(np.asarray(dt_proj_b, np.float32).reshape(DT, 128).T),
        eyeb=np.eye(128, dtype=ml_dtypes.bfloat16),
        eyef=np.eye(128, dtype=np.float32),
        ones_r=np.ones((1, 128), np.float32),
    )


def _mk_masks(cfg):
    NR, PB = cfg["NRANK"], cfg["PERB"]
    msks = []
    for j in range(NR):
        b, pos = j // PB, j % PB
        m = np.zeros((1, NR), np.float32)
        for r in range(NR):
            if r // PB == b and r % PB < pos:
                m[0, r] = 1.0
        msks.append(m)
    return msks


def kernel(L0_attn_x, L0_attn_A_log, L0_attn_D, L0_attn_x_proj_w,
           L0_attn_dt_proj_w, L0_attn_dt_proj_b, L0_ffn_gate_q, L0_ffn_gate_s,
           L0_ffn_up_q, L0_ffn_up_s, L0_ffn_down_q, L0_ffn_down_s):
    cfg = FULL
    NR = cfg["NRANK"]
    TOK = cfg["CH"] * cfg["TC"]
    H = 128 * cfg["HT"]
    x = np.ascontiguousarray(np.asarray(L0_attn_x, np.float32).reshape(-1, H))
    shared = _prep_shared(L0_attn_A_log, L0_attn_D, L0_attn_x_proj_w,
                          L0_attn_dt_proj_w, L0_attn_dt_proj_b, L0_ffn_gate_q,
                          L0_ffn_gate_s, L0_ffn_up_q, L0_ffn_up_s,
                          L0_ffn_down_q, L0_ffn_down_s, cfg)
    msks = _mk_masks(cfg)
    in_maps = []
    for c_ in range(NR):
        im = dict(shared)
        im["xs"] = np.ascontiguousarray(x[c_ * TOK:(c_ + 1) * TOK])
        im["msk"] = msks[c_]
        im["imsk"] = np.ascontiguousarray(1.0 - msks[c_])
        in_maps.append(im)
    nc = build_nc(cfg)
    res = run_bass_kernel_spmd(nc, in_maps, core_ids=list(range(NR)))
    outs = [np.asarray(r["out"], np.float32) for r in res.results]
    return np.concatenate(outs, 0).reshape(np.asarray(L0_attn_x).shape)


# version 09
# speedup vs baseline: 1.0906x; 1.0906x over previous
import numpy as np
import concourse.bass as bass
import concourse.mybir as mybir
from concourse.bass_utils import run_bass_kernel_spmd

F32 = mybir.dt.float32
BF16 = mybir.dt.bfloat16
I32 = mybir.dt.int32
AF = mybir.ActivationFunctionType
ALU = mybir.AluOpType
AXX = mybir.AxisListType.X

FULL = dict(TC=512, CH=2, HT=32, IT=112, NS=16, G=2, R=128, NRANK=8, PERB=4)

V, S, T, Q, Y = "vector", "scalar", "tensor", "gpsimd", "sync"
CHAINED = (V, S, Q)


class Plan:
    ENGS = (Y, V, S, T, Q)

    def __init__(self, nc):
        self.nc = nc
        self.ops = {e: [] for e in self.ENGS}
        self.seq = {e: [nc.alloc_semaphore(name=f"seq_{e}"), 0] for e in self.ENGS}
        self.dsem = {}

    def sem(self, name):
        if name not in self.dsem:
            self.dsem[name] = [self.nc.alloc_semaphore(name=f"d_{name}"), 0]
        return self.dsem[name]

    def dtot(self, name):
        return self.dsem[name][1] if name in self.dsem else 0

    def op(self, eng, emit):
        s = self.seq[eng]
        if eng in CHAINED and s[1] > 0:
            self.ops[eng].append(lambda e, sm=s[0], v=s[1]: e.wait_ge(sm, v))
        s[1] += 1
        self.ops[eng].append(lambda e, emit=emit, sm=s[0]: emit(e).then_inc(sm, 1))
        return s[1]

    def raw(self, eng, emit):
        self.ops[eng].append(emit)

    def wait(self, eng, sem_pair, val):
        if val and val > 0:
            self.ops[eng].append(lambda e, sm=sem_pair[0], v=val: e.wait_ge(sm, v))

    def wseq(self, eng, prod, val):
        self.wait(eng, self.seq[prod], val)

    def wsem(self, eng, name, val):
        if name in self.dsem:
            self.wait(eng, self.dsem[name], val)

    def dma(self, q, emit, semname):
        s = self.sem(semname)
        s[1] += 16
        self.ops[q].append(lambda e, emit=emit, sm=s[0]: emit(e).then_inc(sm, 16))
        return s[1]

    def barrier(self):
        snap = [(self.seq[e][0], self.seq[e][1]) for e in self.ENGS]
        snap += [(v[0], v[1]) for v in self.dsem.values()]
        for e in self.ENGS:
            for sm, v in snap:
                if v > 0:
                    self.ops[e].append(lambda en, sm=sm, v=v: en.wait_ge(sm, v))

    def emit(self):
        nc = self.nc
        with nc.Block() as block:
            @block.sync
            def _(e):
                for f in self.ops[Y]:
                    f(e)

            @block.vector
            def _(e):
                for f in self.ops[V]:
                    f(e)

            @block.scalar
            def _(e):
                for f in self.ops[S]:
                    f(e)

            @block.tensor
            def _(e):
                for f in self.ops[T]:
                    f(e)

            @block.gpsimd
            def _(e):
                for f in self.ops[Q]:
                    f(e)


def build_nc(cfg):
    TC, CH, HT, IT, NS, G, R, NRANK = (cfg[k] for k in
                                       ("TC", "CH", "HT", "IT", "NS", "G", "R", "NRANK"))
    H = 128 * HT
    DT = HT
    NG = DT // G
    TT = TC // 128
    SB8 = H // 512
    DN = DT * NS
    HC = H // 512

    nc = bass.Bass("TRN2", target_bir_lowering=False)
    p = Plan(nc)

    xs = nc.dram_tensor("xs", [CH * TC, H], F32, kind="ExternalInput").ap()
    gblk = nc.dram_tensor("gblk", [IT, 128, H], I32, kind="ExternalInput").ap()
    ublk = nc.dram_tensor("ublk", [IT, 128, H], I32, kind="ExternalInput").ap()
    dblk = nc.dram_tensor("dblk", [IT, 128, H], I32, kind="ExternalInput").ap()
    gsr = nc.dram_tensor("gsr", [IT, H], F32, kind="ExternalInput").ap()
    usr = nc.dram_tensor("usr", [IT, H], F32, kind="ExternalInput").ap()
    dsr = nc.dram_tensor("dsr", [IT, H], F32, kind="ExternalInput").ap()
    xpw = nc.dram_tensor("xpw", [128, HT * 160], F32, kind="ExternalInput").ap()
    dtw = nc.dram_tensor("dtw", [R, H], F32, kind="ExternalInput").ap()
    alog = nc.dram_tensor("alog", [128, DN], F32, kind="ExternalInput").ap()
    dvec = nc.dram_tensor("dvec", [128, DT], F32, kind="ExternalInput").ap()
    dtbb = nc.dram_tensor("dtbb", [128, DT], F32, kind="ExternalInput").ap()
    eyeb = nc.dram_tensor("eyeb", [128, 128], BF16, kind="ExternalInput").ap()
    eyef = nc.dram_tensor("eyef", [128, 128], F32, kind="ExternalInput").ap()
    msk = nc.dram_tensor("msk", [1, NRANK], F32, kind="ExternalInput").ap()
    imsk = nc.dram_tensor("imsk", [1, NRANK], F32, kind="ExternalInput").ap()
    ones_r = nc.dram_tensor("ones_r", [1, 128], F32, kind="ExternalInput").ap()
    out = nc.dram_tensor("out", [CH * TC, H], F32, kind="ExternalOutput").ap()
    DBG = cfg.get("DEBUG", 0)
    if DBG:
        dbg_yg = nc.dram_tensor("dbg_yg", [128, G * TC], F32, kind="ExternalOutput").ap()
        dbg_ystg = nc.dram_tensor("dbg_ystg", [128, G * 128 * TT], F32, kind="ExternalOutput").ap()
        dbg_ndt = nc.dram_tensor("dbg_ndt", [128, G * TC], F32, kind="ExternalOutput").ap()
        dbg_bc = nc.dram_tensor("dbg_bc", [32, TC], F32, kind="ExternalOutput").ap()
        dbg_bexp = nc.dram_tensor("dbg_bexp", [128, 2 * TC], F32, kind="ExternalOutput").ap()

    wg_scr = nc.dram_tensor("wg_scr", [IT, 128, H], BF16, kind="Internal").ap()
    wu_scr = nc.dram_tensor("wu_scr", [IT, 128, H], BF16, kind="Internal").ap()
    wd_scr = nc.dram_tensor("wd_scr", [IT, 128, H], BF16, kind="Internal").ap()
    int_scr = nc.dram_tensor("int_scr", [CH, IT, 128, TC], BF16, kind="Internal").ap()
    ndt_scr = nc.dram_tensor("ndt_scr", [CH, DT, 128, TC], F32, kind="Internal").ap()
    bc_scr = nc.dram_tensor("bc_scr", [CH, 2 * NS, TC], BF16, kind="Internal").ap()
    cc_in = nc.dram_tensor("cc_in", [128, 2 * DN], F32, kind="Internal").ap()
    cc_out = nc.dram_tensor("cc_out", [NRANK * 128, 2 * DN], F32, kind="Internal").ap()
    gsr_b = nc.dram_tensor("gsr_b", [IT, H], BF16, kind="Internal").ap()
    usr_b = nc.dram_tensor("usr_b", [IT, H], BF16, kind="Internal").ap()
    dsr_b = nc.dram_tensor("dsr_b", [IT, H], BF16, kind="Internal").ap()

    def sb(name, shape, dt):
        g = nc.sbuf_tensor(name, shape, dt)
        return g.__enter__()

    xpw_b = sb("xpw_b", [128, HT * 160], BF16)
    dtw_b = sb("dtw_b", [R, H], BF16)
    apos = sb("apos", [128, DN], F32)
    dv_sb = sb("dv_sb", [128, DT], F32)
    dtb_sb = sb("dtb_sb", [128, DT], F32)
    rsumb = sb("rsumb", [128, DT], F32)
    carry = sb("carry", [128, DN], F32)
    cdtc = sb("cdtc", [128, DT], F32)
    h_in = sb("h_in", [128, DN], F32)
    mske = sb("mske", [128, NRANK], F32)
    imske = sb("imske", [128, NRANK], F32)
    eye_sb = sb("eye_sb", [128, 128], BF16)
    eyf_sb = sb("eyf_sb", [128, 128], F32)
    ones_sb = sb("ones_sb", [128, 128], BF16)
    zeros_sb = sb("zeros_sb", [128, TC], BF16)
    xnT = sb("xnT", [128, HT * TC], BF16)
    dt_inT = sb("dt_inT", [R, TC], BF16)
    ndtg = sb("ndtg", [128, G * TC], F32)
    ndtxg = sb("ndtxg", [128, G * TC], BF16)
    sp2 = sb("sp2", [128, G * TC], F32)
    yg = sb("yg", [128, G * TC], F32)
    yg2 = sb("yg2", [128, G * TC], F32)
    spR = yg
    sp1 = sb("sp1", [128, G * TC], F32)
    rtmp = sb("rtmp", [128, 1], F32)
    RD = 4
    bexp = sb("bexp", [128, RD * TC], BF16)
    cexp = sb("cexp", [128, RD * TC], BF16)
    dA2 = sb("dA2", [128, 8 * TC], BF16)
    dA2r = sb("dA2r", [128, 2 * TC], F32)
    dbx = sb("dbx", [128, TC], BF16)
    hbuf = sb("hbuf", [128, TC], BF16)
    tmpb = sb("tmpb", [128, TC], BF16)
    bc32 = sb("bc32", [32, TC], F32)
    mv = sb("mv", [128, 2], F32)
    bn6 = sb("bn6", [128, 6 * SB8], F32)
    rstd = sb("rstd", [128, 1], F32)
    nmr = sb("nmr", [128, 1], F32)
    ystg = sb("ystg", [128, G * 128 * TT], F32)
    stgx = sb("stgx", [128, H], F32)
    stgx_b = stgx[:].bitcast(BF16)
    wg3 = stgx_b[:, 0:H]
    wu3 = stgx_b[:, H:2 * H]
    ccsb = stgx[:, 0:2 * DN]
    rkb = stgx[:, 2 * DN:4 * DN]
    ch1 = stgx[:, 4 * DN:5 * DN]
    ch2 = stgx[:, 5 * DN:6 * DN]
    alog_f = stgx[:, 6 * DN:7 * DN]
    xn_t = sb("xn_t", [128, H], BF16)
    wgb = sb("wgb", [128, 2 * H], BF16)
    wub = sb("wub", [128, 2 * H], BF16)
    wdb = sb("wdb", [128, H], BF16)
    sxg = sb("sxg", [128, H], BF16)
    sxd = sxg
    tanh_t = sb("tanh_t", [128, TC], BF16)
    tplus = sb("tplus", [128, TC], BF16)
    ghalf = tplus
    intT = sb("intT", [128, TC], BF16)
    intR = sb("intR", [128, 4 * TC], BF16)
    dsl = sb("dsl", [128, 4 * 1024], BF16)
    dstg = sb("dstg", [128, 1024], F32)
    eps_c = sb("eps_c", [128, 1], F32)
    one_c = sb("one_c", [128, 1], F32)

    ps = {}
    psg = {}
    porder = []

    puid = [0]
    _PSB = {F32: 512, BF16: 1024, I32: 512}

    def palloc(name, shape, dt):
        puid[0] += 1
        bank_elems = _PSB[dt]
        padded = [shape[0], ((shape[1] + bank_elems - 1) // bank_elems) * bank_elems]
        g = nc.psum_tensor(f"{name}_{puid[0]}", padded, dt)
        psg[name] = g
        h = g.__enter__()
        ps[name] = h[:, 0:shape[1]] if padded[1] != shape[1] else h
        porder.append(name)

    def pfree(*names):
        names = set(names)
        while names:
            n = porder.pop()
            assert n in names, f"psum free order: {n} not in {names}"
            names.discard(n)
            psg[n].__exit__(None, None, None)
            del psg[n], ps[n]

    last = {}

    cl = "ld_const"
    p.dma(Y, lambda e: e.dma_start(out=eye_sb[:], in_=eyeb), cl)
    p.dma(Y, lambda e: e.dma_start(out=eyf_sb[:], in_=eyef), cl)
    p.dma(Y, lambda e: e.dma_start(out=dv_sb[:], in_=dvec), cl)
    p.dma(Y, lambda e: e.dma_start(out=dtb_sb[:], in_=dtbb), cl)
    p.dma(Y, lambda e: e.dma_start(out=mske[:], in_=msk.broadcast_to([128, NRANK])), cl)
    p.dma(Y, lambda e: e.dma_start(out=imske[:], in_=imsk.broadcast_to([128, NRANK])), cl)
    p.dma(Y, lambda e: e.dma_start(out=alog_f[:], in_=alog), cl)
    gcl = "ld_gconst"
    p.dma(Q, lambda e: e.dma_start(out=xpw_b[:], in_=xpw), gcl)
    p.dma(Q, lambda e: e.dma_start(out=dtw_b[:], in_=dtw), gcl)
    n_gcl = p.dtot(gcl)
    p.dma(Q, lambda e: e.dma_start(out=gsr_b, in_=gsr), "srcast")
    p.dma(Q, lambda e: e.dma_start(out=usr_b, in_=usr), "srcast")
    p.dma(Q, lambda e: e.dma_start(out=dsr_b, in_=dsr), "srcast")
    n_srcast = p.dtot("srcast")

    p.wsem(V, cl, p.dtot(cl))
    p.op(V, lambda e: e.memset(rsumb[:], 0.0))
    p.op(V, lambda e: e.memset(h_in[:], 0.0))
    p.op(V, lambda e: e.memset(zeros_sb[:], 0.0))
    p.op(V, lambda e: e.memset(eps_c[:], 1e-5))
    p.op(V, lambda e: e.memset(ones_sb[:], 1.0))
    n_onec = p.op(V, lambda e: e.memset(one_c[:], 1.0))
    p.wsem(S, cl, p.dtot(cl))
    n_apos = p.op(S, lambda e: e.activation(apos[:], alog_f[:], AF.Exp))
    n_ones = p.seq[V][1]

    ldx, base, yadd, dadd, cadd = "ld_x", "st_base", "st_yadd", "st_dadd", "st_cadd"
    wcast, wplain, srs = "ld_wcast", "ld_wplain", "ld_srow"
    intw, intr, dslr = "st_intw", "ld_intr", "ld_dsl"
    ndtw, ndtrd, repl = "st_ndt", "ld_ndt", "ld_repl"
    bcw, scrw, ccw, ccr = "st_bc", "st_wscr", "ccw", "ccr"

    def norm_phase(ch):
        palloc("trn", [128, 128], BF16)
        palloc("trn1", [128, 128], BF16)
        if ch == 0:
            p.wseq(Y, S, n_apos)
        for tt in range(TT):
            par = 0
            stg = stgx[:, 0:H]
            w = last.get(("stg_war", par))
            if w:
                p.wait(Y, p.dsem[base + str(par)], w)
            p.dma(Y, lambda e, stg=stg, r0=ch * TC + tt * 128:
                  e.dma_start(out=stg, in_=xs[r0:r0 + 128, :]), ldx + str(par))
            nx = p.dtot(ldx + str(par))
            p.wsem(V, ldx + str(par), nx)
            for c in range(SB8):
                p.op(V, lambda e, stg=stg, c=c: e.bn_stats(
                    bn6[:, c * 6:(c + 1) * 6], stg[:, c * 512:(c + 1) * 512]))
            p.wseq(V, S, last.get(("norm_s",), 0))
            nmv = p.op(V, lambda e: e.bn_aggr(mv[:], bn6[:]))
            p.wseq(S, V, nmv)
            p.wseq(S, V, n_onec)
            p.op(S, lambda e: e.activation(rstd[:], mv[:, 1:2], AF.Ln, bias=eps_c[:]))
            nr = p.op(S, lambda e: e.activation(rstd[:], rstd[:], AF.Exp, scale=-0.5))
            p.wseq(V, S, nr)
            nn = p.op(V, lambda e: e.scalar_tensor_tensor(
                nmr[:], mv[:, 0:1], -1.0, rstd[:], ALU.mult, ALU.mult))
            p.wseq(V, T, last.get(("xn_T_war",), 0))
            w = last.get((("sxu_war", 0),))
            if w:
                p.wseq(V, w[0], w[1])
            nxn = p.op(V, lambda e, stg=stg: e.tensor_scalar(
                xn_t[:], stg, rstd[:], nmr[:], ALU.mult, ALU.add))
            last[("norm_s",)] = p.seq[S][1]
            p.wseq(Y, V, nxn)
            p.dma(Y, lambda e, stg=stg, r0=ch * TC + tt * 128:
                  e.dma_start(out=out[r0:r0 + 128, :], in_=stg), base + str(par))
            last[("stg_war", par)] = p.dtot(base + str(par))
            for ht in range(HT):
                pnm = "trn" if ht % 2 == 0 else "trn1"
                p.wseq(T, V, nxn)
                p.wseq(T, S, last.get(("trn_ev", pnm), 0))
                ntr = p.op(T, lambda e, ht=ht, ph=ps[pnm]: e.transpose(
                    ph[:], xn_t[:, ht * 128:(ht + 1) * 128], eye_sb[:]))
                p.wseq(S, T, ntr)
                nev = p.op(S, lambda e, ht=ht, tt=tt, ph=ps[pnm]: e.copy(
                    xnT[:, ht * TC + tt * 128: ht * TC + tt * 128 + 128],
                    ph[:]))
                last[("trn_ev", pnm)] = nev
            last[("xn_T_war",)] = p.seq[T][1]
        last[("xnT", ch)] = p.seq[S][1]
        pfree("trn", "trn1")

    def xp_phase(ch):
        palloc("xp", [128, TC], F32)
        palloc("xp2", [32, TC], F32)
        nxnt = last[("xnT", ch)]
        p.wseq(T, S, nxnt)
        p.wsem(T, gcl, n_gcl)
        nxp = None
        for ht in range(HT):
            st, spp = (ht == 0), (ht == HT - 1)
            p.raw(T, lambda e, ht=ht, st=st, spp=spp, ph=ps["xp"]: e.matmul(
                ph[:], xpw_b[:, ht * 160: ht * 160 + 128],
                xnT[:, ht * TC:(ht + 1) * TC], start=st, stop=spp))
            em = lambda e, ht=ht, st=st, spp=spp, ph=ps["xp2"]: e.matmul(
                ph[:], xpw_b[:, ht * 160 + 128: ht * 160 + 160],
                xnT[:, ht * TC:(ht + 1) * TC], start=st, stop=spp)
            if spp:
                nxp = p.op(T, em)
            else:
                p.raw(T, em)
        p.wseq(S, T, nxp)
        ndti = p.op(S, lambda e, ph=ps["xp"]: e.copy(dt_inT[:], ph[:]))
        last[("dt_inT", ch)] = ndti
        p.wseq(V, T, nxp)
        nbc = p.op(V, lambda e, ph=ps["xp2"]: e.tensor_copy(bc32[:], ph[:]))
        p.wseq(Q, V, nbc)
        p.dma(Q, lambda e, ch=ch: e.dma_start(out=bc_scr[ch], in_=bc32[:]), bcw)
        pfree("xp", "xp2")
        p.wseq(T, S, ndti)
        p.wseq(T, V, nbc)

    def dt_group(ch, g):
        SUB = cfg.get("AMSUB", 9)
        ndti = last[("dt_inT", ch)]
        s_entry = p.seq[S][1]
        t_entry = last.get(("ytr_T",), 0)
        nstt = []
        for k in range(G):
            d = g * G + k
            pnm = "dt" if k % 2 == 0 else "dt1"
            p.wseq(T, S, last.get(("psdt_s", pnm), 0))
            p.wseq(T, V, last.get(("psdt_v", pnm), 0))
            p.wseq(T, S, ndti)
            p.wsem(T, gcl, n_gcl)
            nmm = p.op(T, lambda e, d=d, ph=ps[pnm]: e.matmul(
                ph[:], dtw_b[:, d * 128:(d + 1) * 128], dt_inT[:],
                start=True, stop=True))
            p.wseq(S, T, nmm)
            p.wseq(S, V, last.get(("sp1_war",), 0))
            na = p.op(S, lambda e, k=k, d=d, ph=ps[pnm]: e.activation(
                sp1[:, k * TC:(k + 1) * TC], ph[:], AF.Abs,
                bias=dtb_sb[:, d:d + 1]))
            p.wseq(V, T, nmm)
            p.wseq(V, S, na)
            p.wseq(V, T, t_entry)
            nrl = p.op(V, lambda e, k=k, d=d, ph=ps[pnm]: e.tensor_scalar(
                spR[:, k * TC:(k + 1) * TC], ph[:], dtb_sb[:, d:d + 1], 0.0,
                ALU.add, ALU.max))
            last[("psdt_s", pnm)] = na
            last[("psdt_v", pnm)] = nrl
        if SUB < 2:
            last[("sp1_war",)] = p.seq[V][1]
            last[("ndtg", ch, g)] = p.seq[V][1]
            return
        for k in range(G):
            p.op(S, lambda e, k=k: e.activation(
                sp2[:, k * TC:(k + 1) * TC], sp1[:, k * TC:(k + 1) * TC],
                AF.Exp, scale=-1.0))
        for k in range(G):
            nl = p.op(S, lambda e, k=k: e.activation(
                sp1[:, k * TC:(k + 1) * TC], sp2[:, k * TC:(k + 1) * TC],
                AF.Ln, bias=one_c[:]))
        if SUB < 3:
            last[("sp1_war",)] = p.seq[V][1]
            last[("ndtg", ch, g)] = p.seq[V][1]
            return
        for k in range(G):
            d = g * G + k
            p.wseq(V, S, nl)
            p.wseq(V, S, s_entry)
            p.wsem(V, ndtw, last.get(("ndtw_war",), 0))
            n1 = p.op(V, lambda e, k=k: e.scalar_tensor_tensor(
                ndtg[:, k * TC:(k + 1) * TC], sp1[:, k * TC:(k + 1) * TC], -1.0,
                spR[:, k * TC:(k + 1) * TC], ALU.mult, ALU.subtract))
            nstt.append(n1)
            p.op(V, lambda e, k=k, d=d: e.tensor_mul(
                ndtxg[:, k * TC:(k + 1) * TC], ndtg[:, k * TC:(k + 1) * TC],
                xnT[:, d * TC:(d + 1) * TC]))
            p.wseq(V, T, t_entry)
            p.op(V, lambda e, k=k, d=d: e.tensor_scalar(
                yg[:, k * TC:(k + 1) * TC], xnT[:, d * TC:(d + 1) * TC],
                dv_sb[:, d:d + 1], None, ALU.mult))
            p.op(V, lambda e, k=k: e.reduce_sum(
                rtmp[:], ndtg[:, k * TC:(k + 1) * TC], AXX))
            p.op(V, lambda e, d=d: e.tensor_add(
                rsumb[:, d:d + 1], rsumb[:, d:d + 1], rtmp[:]))
            if SUB >= 4:
                p.wseq(Y, V, n1)
                p.dma(Y, lambda e, ch=ch, d=d, k=k: e.dma_start(
                    out=ndt_scr[ch, d], in_=ndtg[:, k * TC:(k + 1) * TC]), ndtw)
        last[("sp1_war",)] = nstt[-1]
        last[("ndtw_war",)] = p.dtot(ndtw)
        last[("ndtg", ch, g)] = p.seq[V][1]

    def scan_slice(ch, g, n, slice_idx, n_slices):
        par = slice_idx % RD
        rtarget = 32 * (slice_idx // RD + 1) + last[("repl0", ch, par)]
        p.wsem(V, repl + str(par), rtarget)
        p.wsem(S, repl + str(par), rtarget)
        for k in range(G):
            d = g * G + k
            apar = (slice_idx * G + k) % 8
            p.wseq(S, V, last[("ndtg", ch, g)])
            p.wseq(S, V, last.get(("dA_used", apar), 0))
            nda = p.op(S, lambda e, k=k, d=d, n=n, apar=apar: e.activation(
                dA2[:, apar * TC:(apar + 1) * TC], ndtg[:, k * TC:(k + 1) * TC],
                AF.Exp, scale=apos[:, d * NS + n: d * NS + n + 1]))
            p.op(V, lambda e, k=k, par=par: e.scalar_tensor_tensor(
                dbx[:], bexp[:, par * TC:(par + 1) * TC], -1.0,
                ndtxg[:, k * TC:(k + 1) * TC], ALU.mult, ALU.mult))
            p.wseq(V, S, nda)
            init = 0.0 if ch == 0 else carry[:, d * NS + n: d * NS + n + 1]
            nh = p.op(V, lambda e, apar=apar, init=init: e.tensor_tensor_scan(
                hbuf[:], dA2[:, apar * TC:(apar + 1) * TC], dbx[:], init,
                ALU.mult, ALU.add))
            last[("dA_used", apar)] = nh
            p.op(V, lambda e, d=d, n=n: e.tensor_copy(
                carry[:, d * NS + n: d * NS + n + 1], hbuf[:, TC - 1:TC]))
            p.op(V, lambda e, par=par: e.tensor_mul(
                tmpb[:], hbuf[:], cexp[:, par * TC:(par + 1) * TC]))
            p.op(V, lambda e, k=k: e.tensor_add(
                yg[:, k * TC:(k + 1) * TC], yg[:, k * TC:(k + 1) * TC], tmpb[:]))
        last[("slice_v", par)] = p.seq[V][1]
        nxt = slice_idx + RD - 1
        if nxt < n_slices:
            npar = nxt % RD
            nn2 = nxt % NS
            p.wseq(Y, V, last.get(("slice_v", npar), 0))
            p.dma(Y, lambda e, ch=ch, n=nn2, npar=npar: e.dma_start(
                out=bexp[:, npar * TC:(npar + 1) * TC],
                in_=bc_scr[ch, n:n + 1, :].broadcast_to([128, TC])), repl + str(npar))
            p.dma(Y, lambda e, ch=ch, n=nn2, npar=npar: e.dma_start(
                out=cexp[:, npar * TC:(npar + 1) * TC],
                in_=bc_scr[ch, NS + n: NS + n + 1, :].broadcast_to([128, TC])),
                repl + str(npar))

    def y_flush(ch, g, which, trps, ygb=None):
        if ygb is None:
            ygb = yg
        if DBG and ch == 0 and g == 0 and which == yadd:
            nv = p.seq[V][1]
            p.wseq(Y, V, nv)
            p.dma(Y, lambda e: e.dma_start(out=dbg_yg, in_=yg[:]), "dbgy")
            p.dma(Y, lambda e: e.dma_start(out=dbg_ndt, in_=ndtg[:]), "dbgy")
            p.dma(Y, lambda e: e.dma_start(out=dbg_bc, in_=bc32[:]), "dbgy")
            p.dma(Y, lambda e: e.dma_start(out=dbg_bexp, in_=bexp[:]), "dbgy")
        """Transpose yg (f32) and DMA-accumulate into out at cols g*G*128.."""
        nyg = p.seq[V][1]
        gate = last.get(("flush_gate", which, ch))
        for k in range(G):
            for tt in range(TT):
                p.wseq(T, V, nyg)
                p.wseq(T, V, last.get(("ytr_used",), 0))
                ntr = p.op(T, lambda e, k=k, tt=tt, ph=ps[trps], ygb=ygb: e.transpose(
                    ph[:], ygb[:, k * TC + tt * 128: k * TC + tt * 128 + 128],
                    eyf_sb[:]))
                p.wseq(V, T, ntr)
                w = last.get(("ystg_used",))
                if w:
                    p.wait(V, p.dsem[w[0]], w[1])
                nev = p.op(V, lambda e, k=k, tt=tt, ph=ps[trps]: e.tensor_copy(
                    ystg[:, tt * (G * 128) + k * 128: tt * (G * 128) + (k + 1) * 128],
                    ph[:]))
                last[("ytr_used",)] = nev
        last[("ytr_T",)] = p.seq[T][1]
        nys = p.seq[V][1]
        for tt in range(TT):
            p.wseq(Q, V, nys)
            p.wsem(Q, base + "0", p.dtot(base + "0"))
            p.wsem(Q, base + "1", p.dtot(base + "1"))
            if gate:
                p.wsem(Q, gate[0], gate[1])
                p.wsem(Q, gate[2], gate[3])
            p.dma(Q, lambda e, ch=ch, g=g, tt=tt: e.dma_start(
                out=out[ch * TC + tt * 128: ch * TC + (tt + 1) * 128,
                        g * G * 128:(g + 1) * G * 128],
                in_=ystg[:, tt * G * 128:(tt + 1) * G * 128],
                accum_op=ALU.add), which)
        last[("ystg_used",)] = (which, p.dtot(which))
        if DBG and ch == 0 and g == 0 and which == yadd:
            p.wseq(Y, V, nys)
            p.dma(Y, lambda e: e.dma_start(out=dbg_ystg, in_=ystg[:]), "dbgy")

    def sexp_repl(j, table, dst_sb, warkey, semname):
        ww = last.get((warkey,))
        if ww:
            p.wseq(Y, ww[0], ww[1])
        p.wsem(Y, "srcast", n_srcast)
        p.dma(Y, lambda e, j=j, table=table, dst=dst_sb: e.dma_start(
            out=dst, in_=table[j:j + 1, :].broadcast_to([128, H])), semname)
        return p.dtot(semname)

    def wbufs(j):
        par = j % 3
        if par == 2:
            return par, wg3, wu3
        return par, wgb[:, par * H:(par + 1) * H], wub[:, par * H:(par + 1) * H]

    def ffn_pre(ch, j):
        par, wg, wu = wbufs(j)
        ldq = Q if ch == 0 else Y
        p.wait(ldq, p.seq[T], last.get(("wbuf_used", par), 0))
        p.wsem(ldq, scrw + str(par), last.get(("wscr_war", par), 0))
        if par == 2:
            p.wsem(ldq, base + "0", p.dtot(base + "0"))
        if ch == 0:
            p.wseq(ldq, V, last.get(("wdq", par), 0))
            wsem = wcast + str(par)
            p.dma(Q, lambda e, j=j, wg=wg: e.dma_start(out=wg, in_=gblk[j]), wsem)
            p.dma(Q, lambda e, j=j, wu=wu: e.dma_start(out=wu, in_=ublk[j]), wsem)
        else:
            wsem = wplain + str(par)
            p.dma(Y, lambda e, j=j, wg=wg: e.dma_start(out=wg, in_=wg_scr[j]), wsem)
            p.dma(Y, lambda e, j=j, wu=wu: e.dma_start(out=wu, in_=wu_scr[j]), wsem)
        wtot = p.dtot(wsem)
        ndq = 0
        if ch == 0:
            sp2_ = j % 2
            gslot = sxg[:] if sp2_ == 0 else dsl[:, 0:H]
            uslot = xn_t[:] if sp2_ == 0 else wdb[:]
            nse_g = sexp_repl(j, gsr_b, gslot, ("sxg_war", sp2_), "sxrepg" + str(sp2_))
            nse_u = sexp_repl(j, usr_b, uslot, ("sxu_war", sp2_), "sxrepu" + str(sp2_))
            p.wsem(V, wsem, wtot)
            p.wsem(V, "sxrepg" + str(sp2_), nse_g)
            ndq_g = p.op(V, lambda e, wg=wg, gslot=gslot: e.tensor_mul(wg, wg, gslot))
            last[(("sxg_war", sp2_),)] = (V, ndq_g)
            p.wsem(V, "sxrepu" + str(sp2_), nse_u)
            ndq_u = p.op(V, lambda e, wu=wu, uslot=uslot: e.tensor_mul(wu, wu, uslot))
            last[(("sxu_war", sp2_),)] = (V, ndq_u)
            last[("wdq", par)] = ndq_u
            p.wseq(Y, V, ndq_g)
            p.dma(Y, lambda e, j=j, wg=wg: e.dma_start(out=wg_scr[j], in_=wg),
                  scrw + str(par))
            p.wseq(Y, V, ndq_u)
            p.dma(Y, lambda e, j=j, wu=wu: e.dma_start(out=wu_scr[j], in_=wu),
                  scrw + str(par))
            last[("wscr_war", par)] = p.dtot(scrw + str(par))
            ndq = ndq_u
        last[("ffn_pre", j)] = (wsem, wtot, ndq)

    def silu_chain(ch, j):
        pp = j % 2
        ng, nu = last.pop(("mmout", j))
        p.wseq(S, T, ng)
        p.wseq(S, V, last.get(("tanh_war",), 0))
        nth = p.op(S, lambda e, ph=ps[f"g{pp}"]: e.activation(
            tanh_t[:], ph[:], AF.Tanh, scale=0.5))
        p.wseq(V, S, nth)
        ntp = p.op(V, lambda e: e.tensor_scalar(
            tplus[:], tanh_t[:], 0.5, 0.5, ALU.mult, ALU.add))
        last[("tanh_war",)] = ntp
        p.wseq(V, T, ng)
        ngh = p.op(V, lambda e, ph=ps[f"g{pp}"]: e.tensor_mul(ghalf[:], ph[:], tplus[:]))
        last[("psg_s", pp)] = nth
        last[("psg_v", pp)] = ngh
        p.wseq(V, T, nu)
        p.wsem(V, intw + "0", last.get(("intw_war", 0), 0))
        nit = p.op(V, lambda e, ph=ps[f"u{pp}"]: e.tensor_mul(
            intT[:, 0:TC], ph[:], ghalf[:]))
        last[("psu_v", pp)] = nit
        p.wseq(Y, V, nit)
        p.dma(Y, lambda e, ch=ch, j=j: e.dma_start(
            out=int_scr[ch, j], in_=intT[:, 0:TC]), intw + "0")
        last[("intw_war", 0)] = p.dtot(intw + "0")

    def ffn_mm(ch, j):
        par, wg, wu = wbufs(j)
        pp = j % 2
        wsem, wtot, ndq = last.pop(("ffn_pre", j))
        p.wseq(T, S, last[("xnT", ch)])
        if ch == 0:
            p.wseq(T, V, ndq)
        else:
            p.wsem(T, wsem, wtot)
        p.wseq(T, S, last.get(("psg_s", pp), 0))
        p.wseq(T, V, last.get(("psg_v", pp), 0))
        ng = nu = None
        for ht in range(HT):
            em = lambda e, ht=ht, wg=wg, ph=ps[f"g{pp}"]: e.matmul(
                ph[:], wg[:, ht * 128:(ht + 1) * 128],
                xnT[:, ht * TC:(ht + 1) * TC], start=(ht == 0), stop=(ht == HT - 1))
            ng = p.op(T, em) if ht == HT - 1 else p.raw(T, em)
        p.wseq(T, V, last.get(("psu_v", pp), 0))
        for ht in range(HT):
            em = lambda e, ht=ht, wu=wu, ph=ps[f"u{pp}"]: e.matmul(
                ph[:], wu[:, ht * 128:(ht + 1) * 128],
                xnT[:, ht * TC:(ht + 1) * TC], start=(ht == 0), stop=(ht == HT - 1))
            nu = p.op(T, em) if ht == HT - 1 else p.raw(T, em)
        last[("wbuf_used", par)] = p.seq[T][1]
        last[("mmout", j)] = (ng, nu)

    def down_itile(j):
        p.wait(Q, p.seq[Q], last.get(("wdb_used",), 0))
        p.wsem(Q, "wdscr", last.get(("wdb_war",), 0))
        p.dma(Q, lambda e, j=j: e.dma_start(out=wdb[:], in_=dblk[j]), "wdcast")
        wtot = p.dtot("wdcast")
        nse = sexp_repl(j, dsr_b, sxd[:], "sxd_war", "sxrepd")
        p.wsem(Q, "wdcast", wtot)
        p.wsem(Q, "sxrepd", nse)
        ndq = p.op(Q, lambda e: e.tensor_mul(wdb[:], wdb[:], sxd[:]))
        last[("wdb_used",)] = ndq
        last[("sxd_war",)] = (Q, ndq)
        p.wseq(Y, Q, ndq)
        p.dma(Y, lambda e, j=j: e.dma_start(out=wd_scr[j], in_=wdb[:]), "wdscr")
        last[("wdb_war",)] = p.dtot("wdscr")

    def a_phase(ch):
        AM = cfg.get("AMODE", 9)
        norm_phase(ch)
        if AM < 2:
            return
        xp_phase(ch)
        if AM < 3:
            return
        palloc("g0", [128, TC], F32)
        palloc("g1", [128, TC], F32)
        palloc("u0", [128, TC], F32)
        palloc("u1", [128, TC], F32)
        palloc("dt", [128, TC], F32)
        palloc("dt1", [128, TC], F32)
        palloc("try", [128, 128], F32)
        for s0 in range(RD):
            last[("repl0", ch, s0)] = p.dtot(repl + str(s0))
        if AM >= 4:
            p.wsem(Y, bcw, p.dtot(bcw))
            for s0 in range(RD - 1):
                nn0 = s0 % NS
                p.wseq(Y, V, last.get(("slice_v", s0), 0))
                p.dma(Y, lambda e, ch=ch, nn0=nn0, s0=s0: e.dma_start(
                    out=bexp[:, s0 * TC:(s0 + 1) * TC],
                    in_=bc_scr[ch, nn0:nn0 + 1, :].broadcast_to([128, TC])),
                    repl + str(s0))
                p.dma(Y, lambda e, ch=ch, nn0=nn0, s0=s0: e.dma_start(
                    out=cexp[:, s0 * TC:(s0 + 1) * TC],
                    in_=bc_scr[ch, NS + nn0: NS + nn0 + 1,
                               :].broadcast_to([128, TC])),
                    repl + str(s0))
        units = [("ffn", j) for j in range(IT)]
        if ch == 1:
            units = [u for j in range(IT) for u in (("ffn", j), ("down", j))]
        ffn_pre(ch, 0)
        sc = []
        if AM >= 4:
            for g in range(NG):
                sc.append(("dtg", g))
                if AM >= 5:
                    for n in range(NS):
                        sc.append(("scan", g, n))
                if AM >= 6:
                    sc.append(("yfl", g))
        merged = []
        si = 0
        for i, u in enumerate(units):
            merged.append(u)
            want = (i + 1) * len(sc) // len(units)
            while si < want:
                merged.append(sc[si])
                si += 1
        merged.extend(sc[si:])
        slice_idx = 0
        for u in merged:
            if u[0] == "ffn":
                if u[1] + 1 < IT:
                    ffn_pre(ch, u[1] + 1)
                ffn_mm(ch, u[1])
                if u[1] > 0:
                    silu_chain(ch, u[1] - 1)
            elif u[0] == "down":
                down_itile(u[1])
            elif u[0] == "dtg":
                dt_group(ch, u[1])
            elif u[0] == "yfl":
                y_flush(ch, u[1], yadd, "try")
            else:
                scan_slice(ch, u[1], u[2], slice_idx, NG * NS)
                slice_idx += 1
        silu_chain(ch, IT - 1)
        pfree("g0", "g1", "u0", "u1", "dt", "dt1", "try")

    def ag_chain():
        nr = p.seq[V][1]
        for d in range(DT):
            p.wseq(S, V, nr)
            p.op(S, lambda e, d=d: e.activation(
                ccsb[:, DN + d * NS: DN + (d + 1) * NS],
                apos[:, d * NS:(d + 1) * NS], AF.Exp, scale=rsumb[:, d:d + 1]))
        nee = p.seq[S][1]
        ncp = p.op(V, lambda e: e.tensor_copy(ccsb[:, 0:DN], carry[:]))
        p.wseq(Y, V, ncp)
        p.wseq(Y, S, nee)
        p.dma(Y, lambda e: e.dma_start(out=cc_in, in_=ccsb[:]), ccw)
        ccs = p.sem("ccdone")
        p.wsem(Q, ccw, p.dtot(ccw))
        ccs[1] += 1
        p.ops[Q].append(lambda e, sm=ccs[0]: e.collective_compute(
            "AllGather", ALU.bypass, replica_groups=[list(range(NRANK))],
            ins=[cc_in], outs=[cc_out]).then_inc(sm))
        p.wait(Y, ccs, 1)
        prev = 0
        for r in range(NRANK):
            p.wseq(Y, V, prev)
            p.dma(Y, lambda e, r=r: e.dma_start(
                out=rkb[:], in_=cc_out[r * 128:(r + 1) * 128, :]), ccr)
            p.wsem(V, ccr, p.dtot(ccr))
            p.op(V, lambda e, r=r: e.tensor_scalar(
                ch1[:], rkb[:, DN:2 * DN], mske[:, r:r + 1], imske[:, r:r + 1],
                ALU.mult, ALU.add))
            p.op(V, lambda e: e.tensor_mul(ch1[:], h_in[:], ch1[:]))
            p.op(V, lambda e, r=r: e.tensor_scalar(
                ch2[:], rkb[:, 0:DN], mske[:, r:r + 1], None, ALU.mult))
            prev = p.op(V, lambda e: e.tensor_add(h_in[:], ch1[:], ch2[:]))
        last[("chain",)] = prev

    def corr_group(ch, g, slice_base, ygb=None):
        if ygb is None:
            ygb = yg
        ncd = 0
        for k in range(G):
            d = g * G + k
            par = k % 2
            p.wseq(Y, V, last.get(("ndtr_used", par), 0))
            p.dma(Y, lambda e, ch=ch, d=d, par=par: e.dma_start(
                out=dA2r[:, par * TC:(par + 1) * TC], in_=ndt_scr[ch, d]),
                ndtrd + str(par))
            p.wsem(V, ndtrd + str(par), p.dtot(ndtrd + str(par)))
            init = 0.0 if ch == 0 else cdtc[:, d:d + 1]
            p.wseq(V, S, last.get(("corr_s",), 0))
            ncs = p.op(V, lambda e, k=k, par=par, init=init: e.tensor_tensor_scan(
                sp1[:, k * TC:(k + 1) * TC], dA2r[:, par * TC:(par + 1) * TC],
                zeros_sb[:], init, ALU.add, ALU.add))
            last[("ndtr_used", par)] = ncs
            p.op(V, lambda e, k=k, d=d: e.tensor_copy(
                cdtc[:, d:d + 1], sp1[:, k * TC + TC - 1: k * TC + TC]))
            ncd = p.seq[V][1]
        for n in range(NS):
            slice_idx = slice_base + n
            par = slice_idx % 2
            cpar = slice_idx % RD
            rtarget = 16 * (slice_idx // RD + 1) + last[("crepl0", ch, cpar)]
            p.wsem(V, repl + "c" + str(cpar), rtarget)
            p.wsem(S, repl + "c" + str(cpar), rtarget)
            for k in range(G):
                d = g * G + k
                apar = (slice_idx * G + k) % 4
                p.wseq(S, V, ncd)
                p.wseq(S, V, last.get(("ec_used", apar), 0))
                nec = p.op(S, lambda e, k=k, d=d, n=n, apar=apar: e.activation(
                    bexp[:, apar * TC:(apar + 1) * TC], sp1[:, k * TC:(k + 1) * TC],
                    AF.Exp, scale=apos[:, d * NS + n: d * NS + n + 1]))
                last[("corr_s",)] = nec
                p.wseq(V, S, nec)
                p.wseq(V, V, last[("chain",)])
                ntm = p.op(V, lambda e, d=d, n=n, apar=apar, cpar=cpar:
                           e.scalar_tensor_tensor(
                    tmpb[:], bexp[:, apar * TC:(apar + 1) * TC],
                    h_in[:, d * NS + n: d * NS + n + 1],
                    cexp[:, cpar * TC:(cpar + 1) * TC], ALU.mult, ALU.mult))
                last[("ec_used", apar)] = ntm
                if n == 0:
                    p.wseq(V, T, last.get(("ytr_T",), 0))
                    p.op(V, lambda e, k=k, ygb=ygb: e.tensor_copy(
                        ygb[:, k * TC:(k + 1) * TC], tmpb[:]))
                else:
                    p.op(V, lambda e, k=k, ygb=ygb: e.tensor_add(
                        ygb[:, k * TC:(k + 1) * TC], ygb[:, k * TC:(k + 1) * TC],
                        tmpb[:]))
            nxt = slice_idx + RD - 1
            if nxt < NG * NS:
                npar = nxt % RD
                nn2 = nxt % NS
                p.wseq(Y, V, last.get(("cslice_v", npar), 0))
                p.dma(Y, lambda e, ch=ch, nn2=nn2, npar=npar: e.dma_start(
                    out=cexp[:, npar * TC:(npar + 1) * TC],
                    in_=bc_scr[ch, NS + nn2: NS + nn2 + 1,
                               :].broadcast_to([128, TC])), repl + "c" + str(npar))
            last[("cslice_v", cpar)] = p.seq[V][1]

    BW = 1024 if H >= 1024 else 512

    def b_hc_unit(ch, hp):
        for s in range(TT):
            palloc(f"pd{s}", [128, BW], F32)
        nmm = None
        for j in range(IT):
            par = j % 4
            p.wait(Y, p.seq[T], last.get(("intR_used", par), 0))
            p.dma(Y, lambda e, ch=ch, j=j, par=par: e.dma_start(
                out=intR[:, par * TC:(par + 1) * TC], in_=int_scr[ch, j]),
                intr + str(par))
            p.wait(S, p.seq[T], last.get(("intR_used", par), 0))
            p.dma(S, lambda e, j=j, hp=hp, par=par: e.dma_start(
                out=dsl[:, par * BW:(par + 1) * BW],
                in_=wd_scr[j, :, hp * BW:(hp + 1) * BW]), dslr + str(par))
            p.wsem(T, intr + str(par), p.dtot(intr + str(par)))
            p.wsem(T, dslr + str(par), p.dtot(dslr + str(par)))
            if j == 0:
                for s in range(TT):
                    w = last.get(("pd_used", s))
                    if w:
                        p.wseq(T, V, w)
                p.wseq(T, V, last.get(("ytr_used",), 0))
            for s in range(TT):
                for hh in range(BW // 512):
                    em = lambda e, par=par, s=s, j=j, hh=hh, ph=ps[f"pd{s}"]: e.matmul(
                            ph[:, hh * 512:(hh + 1) * 512],
                            intR[:, par * TC + s * 128: par * TC + (s + 1) * 128],
                            dsl[:, par * BW + hh * 512: par * BW + (hh + 1) * 512],
                            start=(j == 0), stop=(j == IT - 1))
                    if s == TT - 1 and hh == BW // 512 - 1:
                        nmm = p.op(T, em)
                    else:
                        p.raw(T, em)
            last[("intR_used", par)] = p.seq[T][1]
        for s in range(TT):
            p.wseq(V, T, nmm)
            p.wsem(V, dadd, last.get(("dstg_war", 0), 0))
            nev = p.op(V, lambda e, s=s, ph=ps[f"pd{s}"]: e.tensor_copy(
                dstg[:, 0:BW], ph[:]))
            last[("pd_used", s)] = nev
            p.wseq(Q, V, nev)
            p.dma(Q, lambda e, ch=ch, hp=hp, s=s: e.dma_start(
                out=out[ch * TC + s * 128: ch * TC + (s + 1) * 128,
                        hp * BW:(hp + 1) * BW],
                in_=dstg[:, 0:BW], accum_op=ALU.add), dadd)
            last[("dstg_war", 0)] = p.dtot(dadd)
        pfree(*[f"pd{s}" for s in range(TT)])

    def b_and_corr(ch):
        for s0 in range(RD):
            last[("crepl0", ch, s0)] = p.dtot(repl + "c" + str(s0))
        for s0 in range(RD - 1):
            nn0 = s0 % NS
            p.wseq(Y, V, last.get(("cslice_v", s0), 0))
            p.dma(Y, lambda e, ch=ch, nn0=nn0, s0=s0: e.dma_start(
                out=cexp[:, s0 * TC:(s0 + 1) * TC],
                in_=bc_scr[ch, NS + nn0: NS + nn0 + 1,
                           :].broadcast_to([128, TC])), repl + "c" + str(s0))
        HP = H // BW
        bi = 0
        pending = []

        def do_flush(fg, gate):
            last[("flush_gate", cadd, ch)] = gate
            palloc("tryc", [128, 128], F32)
            for s in range(TT):
                w = last.get(("pd_used", s))
                if w:
                    p.wseq(T, V, w)
            y_flush(ch, fg, cadd, "tryc", ygb=(yg if fg % 2 == 0 else yg2))
            pfree("tryc")

        for g in range(NG):
            need_hp = ((g + 1) * G * 128 + BW - 1) // BW
            while bi < min(need_hp, HP):
                b_hc_unit(ch, bi)
                bi += 1
                for f in pending:
                    do_flush(*f)
                pending = []
            while pending and pending[0][0] <= g - 2:
                do_flush(*pending.pop(0))
            corr_group(ch, g, g * NS, ygb=(yg if g % 2 == 0 else yg2))
            pending.append((g, (dadd, p.dtot(dadd), dadd, p.dtot(dadd))))
        while bi < HP:
            b_hc_unit(ch, bi)
            bi += 1
        for f in pending:
            do_flush(*f)

    PH = cfg.get("PHASES", 5)
    a_phase(0)
    p.barrier()
    if PH >= 2:
        a_phase(1)
        p.barrier()
    if PH >= 3:
        ag_chain()
    if PH >= 4:
        b_and_corr(0)
        p.barrier()
    if PH >= 5:
        b_and_corr(1)

    for nm, v in p.dsem.items():
        p.wait(Y, v, v[1])
    for e in Plan.ENGS:
        if e != Y:
            p.wait(Y, p.seq[e], p.seq[e][1])

    p.emit()
    return nc



def _prep_shared(A_log, Dv, x_proj_w, dt_proj_w, dt_proj_b,
                 gate_q, gate_s, up_q, up_s, down_q, down_s, cfg):
    HT, IT, NS = cfg["HT"], cfg["IT"], cfg["NS"]
    H = 128 * HT
    DT = HT
    c = np.ascontiguousarray

    def gu_blocks(q):
        return c(np.asarray(q, np.int32).T.reshape(HT, 128, IT, 128)
                 .transpose(2, 1, 0, 3).reshape(IT, 128, H))

    def gu_srows(s):
        return c(np.asarray(s, np.float32).reshape(IT, 128, HT)
                 .transpose(0, 2, 1).reshape(IT, H))

    import ml_dtypes
    return dict(
        gblk=gu_blocks(gate_q), ublk=gu_blocks(up_q),
        dblk=c(np.asarray(down_q, np.int32).T.reshape(IT, 128, H)),
        gsr=gu_srows(gate_s), usr=gu_srows(up_s),
        dsr=c(np.asarray(down_s, np.float32).T),
        xpw=c(np.asarray(x_proj_w, np.float32).T.reshape(HT, 128, 160)
              .transpose(1, 0, 2).reshape(128, HT * 160)),
        dtw=c(np.asarray(dt_proj_w, np.float32).T),
        alog=c(np.asarray(A_log, np.float32).reshape(DT, 128, NS)
               .transpose(1, 0, 2).reshape(128, DT * NS)),
        dvec=c(np.asarray(Dv, np.float32).reshape(DT, 128).T),
        dtbb=c(np.asarray(dt_proj_b, np.float32).reshape(DT, 128).T),
        eyeb=np.eye(128, dtype=ml_dtypes.bfloat16),
        eyef=np.eye(128, dtype=np.float32),
        ones_r=np.ones((1, 128), np.float32),
    )


def _mk_masks(cfg):
    NR, PB = cfg["NRANK"], cfg["PERB"]
    msks = []
    for j in range(NR):
        b, pos = j // PB, j % PB
        m = np.zeros((1, NR), np.float32)
        for r in range(NR):
            if r // PB == b and r % PB < pos:
                m[0, r] = 1.0
        msks.append(m)
    return msks


def kernel(L0_attn_x, L0_attn_A_log, L0_attn_D, L0_attn_x_proj_w,
           L0_attn_dt_proj_w, L0_attn_dt_proj_b, L0_ffn_gate_q, L0_ffn_gate_s,
           L0_ffn_up_q, L0_ffn_up_s, L0_ffn_down_q, L0_ffn_down_s):
    cfg = FULL
    NR = cfg["NRANK"]
    TOK = cfg["CH"] * cfg["TC"]
    H = 128 * cfg["HT"]
    x = np.ascontiguousarray(np.asarray(L0_attn_x, np.float32).reshape(-1, H))
    shared = _prep_shared(L0_attn_A_log, L0_attn_D, L0_attn_x_proj_w,
                          L0_attn_dt_proj_w, L0_attn_dt_proj_b, L0_ffn_gate_q,
                          L0_ffn_gate_s, L0_ffn_up_q, L0_ffn_up_s,
                          L0_ffn_down_q, L0_ffn_down_s, cfg)
    msks = _mk_masks(cfg)
    in_maps = []
    for c_ in range(NR):
        im = dict(shared)
        im["xs"] = np.ascontiguousarray(x[c_ * TOK:(c_ + 1) * TOK])
        im["msk"] = msks[c_]
        im["imsk"] = np.ascontiguousarray(1.0 - msks[c_])
        in_maps.append(im)
    nc = build_nc(cfg)
    res = run_bass_kernel_spmd(nc, in_maps, core_ids=list(range(NR)))
    outs = [np.asarray(r["out"], np.float32) for r in res.results]
    return np.concatenate(outs, 0).reshape(np.asarray(L0_attn_x).shape)


# version 11 (final)
# speedup vs baseline: 1.1357x; 1.0413x over previous
import numpy as np
import concourse.bass as bass
import concourse.mybir as mybir
from concourse.bass_utils import run_bass_kernel_spmd

F32 = mybir.dt.float32
BF16 = mybir.dt.bfloat16
I32 = mybir.dt.int32
AF = mybir.ActivationFunctionType
ALU = mybir.AluOpType
AXX = mybir.AxisListType.X

FULL = dict(TC=512, CH=2, HT=32, IT=112, NS=16, G=2, R=128, NRANK=8, PERB=4)

V, S, T, Q, Y = "vector", "scalar", "tensor", "gpsimd", "sync"
CHAINED = (V, S, Q)


class Plan:
    ENGS = (Y, V, S, T, Q)

    def __init__(self, nc):
        self.nc = nc
        self.ops = {e: [] for e in self.ENGS}
        self.seq = {e: [nc.alloc_semaphore(name=f"seq_{e}"), 0] for e in self.ENGS}
        self.dsem = {}

    def sem(self, name):
        if name not in self.dsem:
            self.dsem[name] = [self.nc.alloc_semaphore(name=f"d_{name}"), 0]
        return self.dsem[name]

    def dtot(self, name):
        return self.dsem[name][1] if name in self.dsem else 0

    def op(self, eng, emit):
        s = self.seq[eng]
        if eng in CHAINED and s[1] > 0:
            self.ops[eng].append(lambda e, sm=s[0], v=s[1]: e.wait_ge(sm, v))
        s[1] += 1
        self.ops[eng].append(lambda e, emit=emit, sm=s[0]: emit(e).then_inc(sm, 1))
        return s[1]

    def raw(self, eng, emit):
        self.ops[eng].append(emit)

    def wait(self, eng, sem_pair, val):
        if val and val > 0:
            self.ops[eng].append(lambda e, sm=sem_pair[0], v=val: e.wait_ge(sm, v))

    def wseq(self, eng, prod, val):
        self.wait(eng, self.seq[prod], val)

    def wsem(self, eng, name, val):
        if name in self.dsem:
            self.wait(eng, self.dsem[name], val)

    def dma(self, q, emit, semname):
        s = self.sem(semname)
        s[1] += 16
        self.ops[q].append(lambda e, emit=emit, sm=s[0]: emit(e).then_inc(sm, 16))
        return s[1]

    def barrier(self):
        snap = [(self.seq[e][0], self.seq[e][1]) for e in self.ENGS]
        snap += [(v[0], v[1]) for v in self.dsem.values()]
        for e in self.ENGS:
            for sm, v in snap:
                if v > 0:
                    self.ops[e].append(lambda en, sm=sm, v=v: en.wait_ge(sm, v))

    def emit(self):
        nc = self.nc
        with nc.Block() as block:
            @block.sync
            def _(e):
                for f in self.ops[Y]:
                    f(e)

            @block.vector
            def _(e):
                for f in self.ops[V]:
                    f(e)

            @block.scalar
            def _(e):
                for f in self.ops[S]:
                    f(e)

            @block.tensor
            def _(e):
                for f in self.ops[T]:
                    f(e)

            @block.gpsimd
            def _(e):
                for f in self.ops[Q]:
                    f(e)


def build_nc(cfg):
    TC, CH, HT, IT, NS, G, R, NRANK = (cfg[k] for k in
                                       ("TC", "CH", "HT", "IT", "NS", "G", "R", "NRANK"))
    H = 128 * HT
    DT = HT
    NG = DT // G
    TT = TC // 128
    SB8 = H // 512
    DN = DT * NS
    HC = H // 512

    nc = bass.Bass("TRN2", target_bir_lowering=False)
    p = Plan(nc)

    xs = nc.dram_tensor("xs", [CH * TC, H], F32, kind="ExternalInput").ap()
    gblk = nc.dram_tensor("gblk", [IT, 128, H], I32, kind="ExternalInput").ap()
    ublk = nc.dram_tensor("ublk", [IT, 128, H], I32, kind="ExternalInput").ap()
    dblk = nc.dram_tensor("dblk", [IT, 128, H], I32, kind="ExternalInput").ap()
    gsr = nc.dram_tensor("gsr", [IT, H], F32, kind="ExternalInput").ap()
    usr = nc.dram_tensor("usr", [IT, H], F32, kind="ExternalInput").ap()
    dsr = nc.dram_tensor("dsr", [IT, H], F32, kind="ExternalInput").ap()
    xpw = nc.dram_tensor("xpw", [128, HT * 160], F32, kind="ExternalInput").ap()
    dtw = nc.dram_tensor("dtw", [R, H], F32, kind="ExternalInput").ap()
    alog = nc.dram_tensor("alog", [128, DN], F32, kind="ExternalInput").ap()
    dvec = nc.dram_tensor("dvec", [128, DT], F32, kind="ExternalInput").ap()
    dtbb = nc.dram_tensor("dtbb", [128, DT], F32, kind="ExternalInput").ap()
    eyeb = nc.dram_tensor("eyeb", [128, 128], BF16, kind="ExternalInput").ap()
    eyef = nc.dram_tensor("eyef", [128, 128], F32, kind="ExternalInput").ap()
    msk = nc.dram_tensor("msk", [1, NRANK], F32, kind="ExternalInput").ap()
    imsk = nc.dram_tensor("imsk", [1, NRANK], F32, kind="ExternalInput").ap()
    ones_r = nc.dram_tensor("ones_r", [1, 128], F32, kind="ExternalInput").ap()
    out = nc.dram_tensor("out", [CH * TC, H], F32, kind="ExternalOutput").ap()
    DBG = cfg.get("DEBUG", 0)
    if DBG:
        dbg_yg = nc.dram_tensor("dbg_yg", [128, G * TC], F32, kind="ExternalOutput").ap()
        dbg_ystg = nc.dram_tensor("dbg_ystg", [128, G * 128 * TT], F32, kind="ExternalOutput").ap()
        dbg_ndt = nc.dram_tensor("dbg_ndt", [128, G * TC], F32, kind="ExternalOutput").ap()
        dbg_bc = nc.dram_tensor("dbg_bc", [32, TC], F32, kind="ExternalOutput").ap()
        dbg_bexp = nc.dram_tensor("dbg_bexp", [128, 2 * TC], F32, kind="ExternalOutput").ap()

    wg_scr = nc.dram_tensor("wg_scr", [IT, 128, H], BF16, kind="Internal").ap()
    wu_scr = nc.dram_tensor("wu_scr", [IT, 128, H], BF16, kind="Internal").ap()
    wd_scr = nc.dram_tensor("wd_scr", [IT, 128, H], BF16, kind="Internal").ap()
    int_scr = nc.dram_tensor("int_scr", [CH, IT, 128, TC], BF16, kind="Internal").ap()
    ndt_scr = nc.dram_tensor("ndt_scr", [CH, DT, 128, TC], F32, kind="Internal").ap()
    bc_scr = nc.dram_tensor("bc_scr", [CH, 2 * NS, TC], BF16, kind="Internal").ap()
    cc_in = nc.dram_tensor("cc_in", [128, 2 * DN], F32, kind="Internal").ap()
    cc_out = nc.dram_tensor("cc_out", [NRANK * 128, 2 * DN], F32, kind="Internal").ap()
    gsr_b = nc.dram_tensor("gsr_b", [IT, H], BF16, kind="Internal").ap()
    usr_b = nc.dram_tensor("usr_b", [IT, H], BF16, kind="Internal").ap()
    dsr_b = nc.dram_tensor("dsr_b", [IT, H], BF16, kind="Internal").ap()

    def sb(name, shape, dt):
        g = nc.sbuf_tensor(name, shape, dt)
        return g.__enter__()

    xpw_b = sb("xpw_b", [128, HT * 160], BF16)
    dtw_b = sb("dtw_b", [R, H], BF16)
    apos = sb("apos", [128, DN], F32)
    dv_sb = sb("dv_sb", [128, DT], F32)
    dtb_sb = sb("dtb_sb", [128, DT], F32)
    rsumb = sb("rsumb", [128, DT], F32)
    carry = sb("carry", [128, DN], F32)
    cdtc = sb("cdtc", [128, DT], F32)
    h_in = sb("h_in", [128, DN], F32)
    mske = sb("mske", [128, NRANK], F32)
    imske = sb("imske", [128, NRANK], F32)
    eye_sb = sb("eye_sb", [128, 128], BF16)
    eyf_sb = sb("eyf_sb", [128, 128], F32)
    ones_sb = sb("ones_sb", [128, 128], BF16)
    zeros_sb = sb("zeros_sb", [128, TC], BF16)
    xnT = sb("xnT", [128, HT * TC], BF16)
    dt_inT = sb("dt_inT", [R, TC], BF16)
    ndtg = sb("ndtg", [128, G * TC], F32)
    ndtxg = sb("ndtxg", [128, G * TC], BF16)
    sp2 = sb("sp2", [128, G * TC], F32)
    yg = sb("yg", [128, G * TC], F32)
    yg2 = sb("yg2", [128, G * TC], F32)
    spR = yg
    sp1 = sb("sp1", [128, G * TC], F32)
    rtmp = sb("rtmp", [128, 1], F32)
    RD = 4
    bexp = sb("bexp", [128, RD * TC], BF16)
    cexp = sb("cexp", [128, RD * TC], BF16)
    dA2 = sb("dA2", [128, 8 * TC], BF16)
    dA2r = sb("dA2r", [128, 2 * TC], F32)
    dbx = sb("dbx", [128, TC], BF16)
    hbuf = sb("hbuf", [128, TC], BF16)
    tmpb = sb("tmpb", [128, TC], BF16)
    bc32 = sb("bc32", [32, TC], F32)
    mv = sb("mv", [128, 2], F32)
    bn6 = sb("bn6", [128, 6 * SB8], F32)
    rstd = sb("rstd", [128, 1], F32)
    nmr = sb("nmr", [128, 1], F32)
    ystg = sb("ystg", [128, G * 128 * TT], F32)
    stgx = sb("stgx", [128, H], F32)
    stgx_b = stgx[:].bitcast(BF16)
    wg3 = stgx_b[:, 0:H]
    wu3 = stgx_b[:, H:2 * H]
    ccsb = stgx[:, 0:2 * DN]
    rkb = stgx[:, 2 * DN:4 * DN]
    ch1 = stgx[:, 4 * DN:5 * DN]
    ch2 = stgx[:, 5 * DN:6 * DN]
    alog_f = stgx[:, 6 * DN:7 * DN]
    xn_t = sb("xn_t", [128, H], BF16)
    wgb = sb("wgb", [128, 2 * H], BF16)
    wub = sb("wub", [128, 2 * H], BF16)
    wdb = sb("wdb", [128, H], BF16)
    sxg = sb("sxg", [128, H], BF16)
    sxd = sxg
    tanh_t = sb("tanh_t", [128, TC], BF16)
    tplus = sb("tplus", [128, TC], BF16)
    ghalf = tplus
    intT = sb("intT", [128, TC], BF16)
    intR = sb("intR", [128, 4 * TC], BF16)
    dsl = sb("dsl", [128, 4 * 1024], BF16)
    dstg = sb("dstg", [128, 1024], F32)
    eps_c = sb("eps_c", [128, 1], F32)
    one_c = sb("one_c", [128, 1], F32)

    ps = {}
    psg = {}
    porder = []

    puid = [0]
    _PSB = {F32: 512, BF16: 1024, I32: 512}

    def palloc(name, shape, dt):
        puid[0] += 1
        bank_elems = _PSB[dt]
        padded = [shape[0], ((shape[1] + bank_elems - 1) // bank_elems) * bank_elems]
        g = nc.psum_tensor(f"{name}_{puid[0]}", padded, dt)
        psg[name] = g
        h = g.__enter__()
        ps[name] = h[:, 0:shape[1]] if padded[1] != shape[1] else h
        porder.append(name)

    def pfree(*names):
        names = set(names)
        while names:
            n = porder.pop()
            assert n in names, f"psum free order: {n} not in {names}"
            names.discard(n)
            psg[n].__exit__(None, None, None)
            del psg[n], ps[n]

    last = {}

    cl = "ld_const"
    p.dma(Y, lambda e: e.dma_start(out=eye_sb[:], in_=eyeb), cl)
    p.dma(Y, lambda e: e.dma_start(out=eyf_sb[:], in_=eyef), cl)
    p.dma(Y, lambda e: e.dma_start(out=dv_sb[:], in_=dvec), cl)
    p.dma(Y, lambda e: e.dma_start(out=dtb_sb[:], in_=dtbb), cl)
    p.dma(Y, lambda e: e.dma_start(out=mske[:], in_=msk.broadcast_to([128, NRANK])), cl)
    p.dma(Y, lambda e: e.dma_start(out=imske[:], in_=imsk.broadcast_to([128, NRANK])), cl)
    p.dma(Y, lambda e: e.dma_start(out=alog_f[:], in_=alog), cl)
    gcl = "ld_gconst"
    p.dma(Q, lambda e: e.dma_start(out=xpw_b[:], in_=xpw), gcl)
    p.dma(Q, lambda e: e.dma_start(out=dtw_b[:], in_=dtw), gcl)
    n_gcl = p.dtot(gcl)
    p.dma(Q, lambda e: e.dma_start(out=gsr_b, in_=gsr), "srcast")
    p.dma(Q, lambda e: e.dma_start(out=usr_b, in_=usr), "srcast")
    p.dma(Q, lambda e: e.dma_start(out=dsr_b, in_=dsr), "srcast")
    n_srcast = p.dtot("srcast")

    p.wsem(V, cl, p.dtot(cl))
    p.op(V, lambda e: e.memset(rsumb[:], 0.0))
    p.op(V, lambda e: e.memset(h_in[:], 0.0))
    p.op(V, lambda e: e.memset(zeros_sb[:], 0.0))
    p.op(V, lambda e: e.memset(eps_c[:], 1e-5))
    p.op(V, lambda e: e.memset(ones_sb[:], 1.0))
    n_onec = p.op(V, lambda e: e.memset(one_c[:], 1.0))
    p.wsem(S, cl, p.dtot(cl))
    n_apos = p.op(S, lambda e: e.activation(apos[:], alog_f[:], AF.Exp))
    n_ones = p.seq[V][1]

    ldx, base, yadd, dadd, cadd = "ld_x", "st_base", "st_yadd", "st_dadd", "st_cadd"
    wcast, wplain, srs = "ld_wcast", "ld_wplain", "ld_srow"
    intw, intr, dslr = "st_intw", "ld_intr", "ld_dsl"
    ndtw, ndtrd, repl = "st_ndt", "ld_ndt", "ld_repl"
    bcw, scrw, ccw, ccr = "st_bc", "st_wscr", "ccw", "ccr"

    def norm_phase(ch):
        palloc("trn", [128, 128], BF16)
        palloc("trn1", [128, 128], BF16)
        if ch == 0:
            p.wseq(Y, S, n_apos)
        for tt in range(TT):
            par = 0
            stg = stgx[:, 0:H]
            w = last.get(("stg_war", par))
            if w:
                p.wait(Y, p.dsem[base + str(par)], w)
            p.dma(Y, lambda e, stg=stg, r0=ch * TC + tt * 128:
                  e.dma_start(out=stg, in_=xs[r0:r0 + 128, :]), ldx + str(par))
            nx = p.dtot(ldx + str(par))
            p.wsem(V, ldx + str(par), nx)
            for c in range(SB8):
                p.op(V, lambda e, stg=stg, c=c: e.bn_stats(
                    bn6[:, c * 6:(c + 1) * 6], stg[:, c * 512:(c + 1) * 512]))
            p.wseq(V, S, last.get(("norm_s",), 0))
            nmv = p.op(V, lambda e: e.bn_aggr(mv[:], bn6[:]))
            p.wseq(S, V, nmv)
            p.wseq(S, V, n_onec)
            p.op(S, lambda e: e.activation(rstd[:], mv[:, 1:2], AF.Ln, bias=eps_c[:]))
            nr = p.op(S, lambda e: e.activation(rstd[:], rstd[:], AF.Exp, scale=-0.5))
            p.wseq(V, S, nr)
            nn = p.op(V, lambda e: e.scalar_tensor_tensor(
                nmr[:], mv[:, 0:1], -1.0, rstd[:], ALU.mult, ALU.mult))
            p.wseq(V, T, last.get(("xn_T_war",), 0))
            w = last.get((("sxu_war", 0),))
            if w:
                p.wseq(V, w[0], w[1])
            nxn = p.op(V, lambda e, stg=stg: e.tensor_scalar(
                xn_t[:], stg, rstd[:], nmr[:], ALU.mult, ALU.add))
            last[("norm_s",)] = p.seq[S][1]
            p.wseq(Y, V, nxn)
            p.dma(Y, lambda e, stg=stg, r0=ch * TC + tt * 128:
                  e.dma_start(out=out[r0:r0 + 128, :], in_=stg), base + str(par))
            last[("stg_war", par)] = p.dtot(base + str(par))
            for ht in range(HT):
                pnm = "trn" if ht % 2 == 0 else "trn1"
                p.wseq(T, V, nxn)
                p.wseq(T, S, last.get(("trn_ev", pnm), 0))
                ntr = p.op(T, lambda e, ht=ht, ph=ps[pnm]: e.transpose(
                    ph[:], xn_t[:, ht * 128:(ht + 1) * 128], eye_sb[:]))
                p.wseq(S, T, ntr)
                nev = p.op(S, lambda e, ht=ht, tt=tt, ph=ps[pnm]: e.copy(
                    xnT[:, ht * TC + tt * 128: ht * TC + tt * 128 + 128],
                    ph[:]))
                last[("trn_ev", pnm)] = nev
            last[("xn_T_war",)] = p.seq[T][1]
        last[("xnT", ch)] = p.seq[S][1]
        pfree("trn", "trn1")

    def xp_phase(ch):
        palloc("xp", [128, TC], F32)
        palloc("xp2", [32, TC], F32)
        nxnt = last[("xnT", ch)]
        p.wseq(T, S, nxnt)
        p.wsem(T, gcl, n_gcl)
        nxp = None
        for ht in range(HT):
            st, spp = (ht == 0), (ht == HT - 1)
            p.raw(T, lambda e, ht=ht, st=st, spp=spp, ph=ps["xp"]: e.matmul(
                ph[:], xpw_b[:, ht * 160: ht * 160 + 128],
                xnT[:, ht * TC:(ht + 1) * TC], start=st, stop=spp))
            em = lambda e, ht=ht, st=st, spp=spp, ph=ps["xp2"]: e.matmul(
                ph[:], xpw_b[:, ht * 160 + 128: ht * 160 + 160],
                xnT[:, ht * TC:(ht + 1) * TC], start=st, stop=spp)
            if spp:
                nxp = p.op(T, em)
            else:
                p.raw(T, em)
        p.wseq(S, T, nxp)
        ndti = p.op(S, lambda e, ph=ps["xp"]: e.copy(dt_inT[:], ph[:]))
        last[("dt_inT", ch)] = ndti
        p.wseq(V, T, nxp)
        nbc = p.op(V, lambda e, ph=ps["xp2"]: e.tensor_copy(bc32[:], ph[:]))
        p.wseq(Q, V, nbc)
        p.dma(Q, lambda e, ch=ch: e.dma_start(out=bc_scr[ch], in_=bc32[:]), bcw)
        pfree("xp", "xp2")
        p.wseq(T, S, ndti)
        p.wseq(T, V, nbc)

    def dt_group(ch, g):
        SUB = cfg.get("AMSUB", 9)
        ndti = last[("dt_inT", ch)]
        s_entry = p.seq[S][1]
        t_entry = last.get(("ytr_T",), 0)
        nstt = []
        for k in range(G):
            d = g * G + k
            pnm = "dt" if k % 2 == 0 else "dt1"
            p.wseq(T, S, last.get(("psdt_s", pnm), 0))
            p.wseq(T, V, last.get(("psdt_v", pnm), 0))
            p.wseq(T, S, ndti)
            p.wsem(T, gcl, n_gcl)
            nmm = p.op(T, lambda e, d=d, ph=ps[pnm]: e.matmul(
                ph[:], dtw_b[:, d * 128:(d + 1) * 128], dt_inT[:],
                start=True, stop=True))
            p.wseq(S, T, nmm)
            p.wseq(S, V, last.get(("sp1_war",), 0))
            na = p.op(S, lambda e, k=k, d=d, ph=ps[pnm]: e.activation(
                sp1[:, k * TC:(k + 1) * TC], ph[:], AF.Abs,
                bias=dtb_sb[:, d:d + 1]))
            p.wseq(V, T, nmm)
            p.wseq(V, S, na)
            p.wseq(V, T, t_entry)
            nrl = p.op(V, lambda e, k=k, d=d, ph=ps[pnm]: e.tensor_scalar(
                spR[:, k * TC:(k + 1) * TC], ph[:], dtb_sb[:, d:d + 1], 0.0,
                ALU.add, ALU.max))
            last[("psdt_s", pnm)] = na
            last[("psdt_v", pnm)] = nrl
        if SUB < 2:
            last[("sp1_war",)] = p.seq[V][1]
            last[("ndtg", ch, g)] = p.seq[V][1]
            return
        for k in range(G):
            p.op(S, lambda e, k=k: e.activation(
                sp2[:, k * TC:(k + 1) * TC], sp1[:, k * TC:(k + 1) * TC],
                AF.Exp, scale=-1.0))
        for k in range(G):
            nl = p.op(S, lambda e, k=k: e.activation(
                sp1[:, k * TC:(k + 1) * TC], sp2[:, k * TC:(k + 1) * TC],
                AF.Ln, bias=one_c[:]))
        if SUB < 3:
            last[("sp1_war",)] = p.seq[V][1]
            last[("ndtg", ch, g)] = p.seq[V][1]
            return
        for k in range(G):
            d = g * G + k
            p.wseq(V, S, nl)
            p.wseq(V, S, s_entry)
            p.wsem(V, ndtw, last.get(("ndtw_war",), 0))
            n1 = p.op(V, lambda e, k=k: e.scalar_tensor_tensor(
                ndtg[:, k * TC:(k + 1) * TC], sp1[:, k * TC:(k + 1) * TC], -1.0,
                spR[:, k * TC:(k + 1) * TC], ALU.mult, ALU.subtract))
            nstt.append(n1)
            p.op(V, lambda e, k=k, d=d: e.tensor_mul(
                ndtxg[:, k * TC:(k + 1) * TC], ndtg[:, k * TC:(k + 1) * TC],
                xnT[:, d * TC:(d + 1) * TC]))
            p.wseq(V, T, t_entry)
            p.op(V, lambda e, k=k, d=d: e.tensor_scalar(
                yg[:, k * TC:(k + 1) * TC], xnT[:, d * TC:(d + 1) * TC],
                dv_sb[:, d:d + 1], None, ALU.mult))
            p.op(V, lambda e, k=k: e.reduce_sum(
                rtmp[:], ndtg[:, k * TC:(k + 1) * TC], AXX))
            p.op(V, lambda e, d=d: e.tensor_add(
                rsumb[:, d:d + 1], rsumb[:, d:d + 1], rtmp[:]))
            if SUB >= 4:
                p.wseq(Y, V, n1)
                p.dma(Y, lambda e, ch=ch, d=d, k=k: e.dma_start(
                    out=ndt_scr[ch, d], in_=ndtg[:, k * TC:(k + 1) * TC]), ndtw)
        last[("sp1_war",)] = nstt[-1]
        last[("ndtw_war",)] = p.dtot(ndtw)
        last[("ndtg", ch, g)] = p.seq[V][1]

    def scan_slice(ch, g, n, slice_idx, n_slices):
        par = slice_idx % RD
        rtarget = 32 * (slice_idx // RD + 1) + last[("repl0", ch, par)]
        p.wsem(V, repl + str(par), rtarget)
        p.wsem(S, repl + str(par), rtarget)
        for k in range(G):
            d = g * G + k
            apar = (slice_idx * G + k) % 8
            p.wseq(S, V, last[("ndtg", ch, g)])
            p.wseq(S, V, last.get(("dA_used", apar), 0))
            nda = p.op(S, lambda e, k=k, d=d, n=n, apar=apar: e.activation(
                dA2[:, apar * TC:(apar + 1) * TC], ndtg[:, k * TC:(k + 1) * TC],
                AF.Exp, scale=apos[:, d * NS + n: d * NS + n + 1]))
            p.op(V, lambda e, k=k, par=par: e.scalar_tensor_tensor(
                dbx[:], bexp[:, par * TC:(par + 1) * TC], -1.0,
                ndtxg[:, k * TC:(k + 1) * TC], ALU.mult, ALU.mult))
            p.wseq(V, S, nda)
            init = 0.0 if ch == 0 else carry[:, d * NS + n: d * NS + n + 1]
            nh = p.op(V, lambda e, apar=apar, init=init: e.tensor_tensor_scan(
                hbuf[:], dA2[:, apar * TC:(apar + 1) * TC], dbx[:], init,
                ALU.mult, ALU.add))
            last[("dA_used", apar)] = nh
            p.op(V, lambda e, d=d, n=n: e.tensor_copy(
                carry[:, d * NS + n: d * NS + n + 1], hbuf[:, TC - 1:TC]))
            p.op(V, lambda e, par=par: e.tensor_mul(
                tmpb[:], hbuf[:], cexp[:, par * TC:(par + 1) * TC]))
            p.op(V, lambda e, k=k: e.tensor_add(
                yg[:, k * TC:(k + 1) * TC], yg[:, k * TC:(k + 1) * TC], tmpb[:]))
        last[("slice_v", par)] = p.seq[V][1]
        nxt = slice_idx + RD - 1
        if nxt < n_slices:
            npar = nxt % RD
            nn2 = nxt % NS
            p.wseq(Y, V, last.get(("slice_v", npar), 0))
            p.dma(Y, lambda e, ch=ch, n=nn2, npar=npar: e.dma_start(
                out=bexp[:, npar * TC:(npar + 1) * TC],
                in_=bc_scr[ch, n:n + 1, :].broadcast_to([128, TC])), repl + str(npar))
            p.dma(Y, lambda e, ch=ch, n=nn2, npar=npar: e.dma_start(
                out=cexp[:, npar * TC:(npar + 1) * TC],
                in_=bc_scr[ch, NS + n: NS + n + 1, :].broadcast_to([128, TC])),
                repl + str(npar))

    def y_flush(ch, g, which, trps, ygb=None):
        if ygb is None:
            ygb = yg
        if DBG and ch == 0 and g == 0 and which == yadd:
            nv = p.seq[V][1]
            p.wseq(Y, V, nv)
            p.dma(Y, lambda e: e.dma_start(out=dbg_yg, in_=yg[:]), "dbgy")
            p.dma(Y, lambda e: e.dma_start(out=dbg_ndt, in_=ndtg[:]), "dbgy")
            p.dma(Y, lambda e: e.dma_start(out=dbg_bc, in_=bc32[:]), "dbgy")
            p.dma(Y, lambda e: e.dma_start(out=dbg_bexp, in_=bexp[:]), "dbgy")
        """Transpose yg (f32) and DMA-accumulate into out at cols g*G*128.."""
        nyg = p.seq[V][1]
        gate = last.get(("flush_gate", which, ch))
        for k in range(G):
            for tt in range(TT):
                p.wseq(T, V, nyg)
                p.wseq(T, V, last.get(("ytr_used",), 0))
                ntr = p.op(T, lambda e, k=k, tt=tt, ph=ps[trps], ygb=ygb: e.transpose(
                    ph[:], ygb[:, k * TC + tt * 128: k * TC + tt * 128 + 128],
                    eyf_sb[:]))
                p.wseq(V, T, ntr)
                w = last.get(("ystg_used",))
                if w:
                    p.wait(V, p.dsem[w[0]], w[1])
                nev = p.op(V, lambda e, k=k, tt=tt, ph=ps[trps]: e.tensor_copy(
                    ystg[:, tt * (G * 128) + k * 128: tt * (G * 128) + (k + 1) * 128],
                    ph[:]))
                last[("ytr_used",)] = nev
        last[("ytr_T",)] = p.seq[T][1]
        nys = p.seq[V][1]
        for tt in range(TT):
            p.wseq(Q, V, nys)
            p.wsem(Q, base + "0", p.dtot(base + "0"))
            p.wsem(Q, base + "1", p.dtot(base + "1"))
            if gate:
                p.wsem(Q, gate[0], gate[1])
                p.wsem(Q, gate[2], gate[3])
            p.dma(Q, lambda e, ch=ch, g=g, tt=tt: e.dma_start(
                out=out[ch * TC + tt * 128: ch * TC + (tt + 1) * 128,
                        g * G * 128:(g + 1) * G * 128],
                in_=ystg[:, tt * G * 128:(tt + 1) * G * 128],
                accum_op=ALU.add), which)
        last[("ystg_used",)] = (which, p.dtot(which))
        if DBG and ch == 0 and g == 0 and which == yadd:
            p.wseq(Y, V, nys)
            p.dma(Y, lambda e: e.dma_start(out=dbg_ystg, in_=ystg[:]), "dbgy")

    def sexp_repl(j, table, dst_sb, warkey, semname):
        ww = last.get((warkey,))
        if ww:
            p.wseq(Y, ww[0], ww[1])
        p.wsem(Y, "srcast", n_srcast)
        p.dma(Y, lambda e, j=j, table=table, dst=dst_sb: e.dma_start(
            out=dst, in_=table[j:j + 1, :].broadcast_to([128, H])), semname)
        return p.dtot(semname)

    def wbufs(j):
        par = j % 3
        if par == 2:
            return par, wg3, wu3
        return par, wgb[:, par * H:(par + 1) * H], wub[:, par * H:(par + 1) * H]

    def ffn_pre(ch, j):
        par, wg, wu = wbufs(j)
        ldq = Q if ch == 0 else Y
        p.wait(ldq, p.seq[T], last.get(("wbuf_used", par), 0))
        p.wsem(ldq, scrw + str(par), last.get(("wscr_war", par), 0))
        if par == 2:
            p.wsem(ldq, base + "0", p.dtot(base + "0"))
        if ch == 0:
            p.wseq(ldq, V, last.get(("wdq", par), 0))
            wsem = wcast + str(par)
            p.dma(Q, lambda e, j=j, wg=wg: e.dma_start(out=wg, in_=gblk[j]), wsem)
            p.dma(Q, lambda e, j=j, wu=wu: e.dma_start(out=wu, in_=ublk[j]), wsem)
        else:
            wsem = wplain + str(par)
            p.dma(Y, lambda e, j=j, wg=wg: e.dma_start(out=wg, in_=wg_scr[j]), wsem)
            p.dma(Y, lambda e, j=j, wu=wu: e.dma_start(out=wu, in_=wu_scr[j]), wsem)
        wtot = p.dtot(wsem)
        ndq = 0
        if ch == 0:
            sp2_ = j % 2
            gslot = sxg[:] if sp2_ == 0 else dsl[:, 0:H]
            uslot = xn_t[:] if sp2_ == 0 else wdb[:]
            nse_g = sexp_repl(j, gsr_b, gslot, ("sxg_war", sp2_), "sxrepg" + str(sp2_))
            nse_u = sexp_repl(j, usr_b, uslot, ("sxu_war", sp2_), "sxrepu" + str(sp2_))
            p.wsem(V, wsem, wtot)
            p.wsem(V, "sxrepg" + str(sp2_), nse_g)
            ndq_g = p.op(V, lambda e, wg=wg, gslot=gslot: e.tensor_mul(wg, wg, gslot))
            last[(("sxg_war", sp2_),)] = (V, ndq_g)
            p.wsem(V, "sxrepu" + str(sp2_), nse_u)
            ndq_u = p.op(V, lambda e, wu=wu, uslot=uslot: e.tensor_mul(wu, wu, uslot))
            last[(("sxu_war", sp2_),)] = (V, ndq_u)
            last[("wdq", par)] = ndq_u
            p.wseq(Y, V, ndq_g)
            p.dma(Y, lambda e, j=j, wg=wg: e.dma_start(out=wg_scr[j], in_=wg),
                  scrw + str(par))
            p.wseq(Y, V, ndq_u)
            p.dma(Y, lambda e, j=j, wu=wu: e.dma_start(out=wu_scr[j], in_=wu),
                  scrw + str(par))
            last[("wscr_war", par)] = p.dtot(scrw + str(par))
            ndq = ndq_u
        last[("ffn_pre", j)] = (wsem, wtot, ndq)

    def silu_chain(ch, j):
        pp = j % 2
        ng, nu = last.pop(("mmout", j))
        p.wseq(S, T, ng)
        p.wseq(S, V, last.get(("tanh_war",), 0))
        nth = p.op(S, lambda e, ph=ps[f"g{pp}"]: e.activation(
            tanh_t[:], ph[:], AF.Tanh, scale=0.5))
        p.wseq(V, S, nth)
        ntp = p.op(V, lambda e: e.tensor_scalar(
            tplus[:], tanh_t[:], 0.5, 0.5, ALU.mult, ALU.add))
        last[("tanh_war",)] = ntp
        p.wseq(V, T, ng)
        ngh = p.op(V, lambda e, ph=ps[f"g{pp}"]: e.tensor_mul(ghalf[:], ph[:], tplus[:]))
        last[("psg_s", pp)] = nth
        last[("psg_v", pp)] = ngh
        p.wseq(V, T, nu)
        p.wsem(V, intw + "0", last.get(("intw_war", 0), 0))
        nit = p.op(V, lambda e, ph=ps[f"u{pp}"]: e.tensor_mul(
            intT[:, 0:TC], ph[:], ghalf[:]))
        last[("psu_v", pp)] = nit
        p.wseq(Y, V, nit)
        p.dma(Y, lambda e, ch=ch, j=j: e.dma_start(
            out=int_scr[ch, j], in_=intT[:, 0:TC]), intw + "0")
        last[("intw_war", 0)] = p.dtot(intw + "0")

    def ffn_mm(ch, j):
        par, wg, wu = wbufs(j)
        pp = j % 2
        wsem, wtot, ndq = last.pop(("ffn_pre", j))
        p.wseq(T, S, last[("xnT", ch)])
        if ch == 0:
            p.wseq(T, V, ndq)
        else:
            p.wsem(T, wsem, wtot)
        p.wseq(T, S, last.get(("psg_s", pp), 0))
        p.wseq(T, V, last.get(("psg_v", pp), 0))
        ng = nu = None
        for ht in range(HT):
            em = lambda e, ht=ht, wg=wg, ph=ps[f"g{pp}"]: e.matmul(
                ph[:], wg[:, ht * 128:(ht + 1) * 128],
                xnT[:, ht * TC:(ht + 1) * TC], start=(ht == 0), stop=(ht == HT - 1))
            ng = p.op(T, em) if ht == HT - 1 else p.raw(T, em)
        p.wseq(T, V, last.get(("psu_v", pp), 0))
        for ht in range(HT):
            em = lambda e, ht=ht, wu=wu, ph=ps[f"u{pp}"]: e.matmul(
                ph[:], wu[:, ht * 128:(ht + 1) * 128],
                xnT[:, ht * TC:(ht + 1) * TC], start=(ht == 0), stop=(ht == HT - 1))
            nu = p.op(T, em) if ht == HT - 1 else p.raw(T, em)
        last[("wbuf_used", par)] = p.seq[T][1]
        last[("mmout", j)] = (ng, nu)

    def down_itile(j):
        p.wait(Q, p.seq[Q], last.get(("wdb_used",), 0))
        p.wsem(Q, "wdscr", last.get(("wdb_war",), 0))
        p.dma(Q, lambda e, j=j: e.dma_start(out=wdb[:], in_=dblk[j]), "wdcast")
        wtot = p.dtot("wdcast")
        nse = sexp_repl(j, dsr_b, sxd[:], "sxd_war", "sxrepd")
        p.wsem(Q, "wdcast", wtot)
        p.wsem(Q, "sxrepd", nse)
        ndq = p.op(Q, lambda e: e.tensor_mul(wdb[:], wdb[:], sxd[:]))
        last[("wdb_used",)] = ndq
        last[("sxd_war",)] = (Q, ndq)
        p.wseq(Y, Q, ndq)
        p.dma(Y, lambda e, j=j: e.dma_start(out=wd_scr[j], in_=wdb[:]), "wdscr")
        last[("wdb_war",)] = p.dtot("wdscr")

    def a_phase(ch):
        AM = cfg.get("AMODE", 9)
        norm_phase(ch)
        if AM < 2:
            return
        xp_phase(ch)
        if AM < 3:
            return
        palloc("g0", [128, TC], F32)
        palloc("g1", [128, TC], F32)
        palloc("u0", [128, TC], F32)
        palloc("u1", [128, TC], F32)
        palloc("dt", [128, TC], F32)
        palloc("dt1", [128, TC], F32)
        palloc("try", [128, 128], F32)
        for s0 in range(RD):
            last[("repl0", ch, s0)] = p.dtot(repl + str(s0))
        if AM >= 4:
            p.wsem(Y, bcw, p.dtot(bcw))
            for s0 in range(RD - 1):
                nn0 = s0 % NS
                p.wseq(Y, V, last.get(("slice_v", s0), 0))
                p.dma(Y, lambda e, ch=ch, nn0=nn0, s0=s0: e.dma_start(
                    out=bexp[:, s0 * TC:(s0 + 1) * TC],
                    in_=bc_scr[ch, nn0:nn0 + 1, :].broadcast_to([128, TC])),
                    repl + str(s0))
                p.dma(Y, lambda e, ch=ch, nn0=nn0, s0=s0: e.dma_start(
                    out=cexp[:, s0 * TC:(s0 + 1) * TC],
                    in_=bc_scr[ch, NS + nn0: NS + nn0 + 1,
                               :].broadcast_to([128, TC])),
                    repl + str(s0))
        units = [("ffn", j) for j in range(IT)]
        if ch == 1:
            units = [u for j in range(IT) for u in (("ffn", j), ("down", j))]
        ffn_pre(ch, 0)
        sc = []
        if AM >= 4:
            for g in range(NG):
                sc.append(("dtg", g))
                if AM >= 5:
                    for n in range(NS):
                        sc.append(("scan", g, n))
                if AM >= 6:
                    sc.append(("yfl", g))
        merged = []
        si = 0
        for i, u in enumerate(units):
            merged.append(u)
            want = (i + 1) * len(sc) // len(units)
            while si < want:
                merged.append(sc[si])
                si += 1
        merged.extend(sc[si:])
        slice_idx = 0
        for u in merged:
            if u[0] == "ffn":
                if u[1] + 1 < IT:
                    ffn_pre(ch, u[1] + 1)
                ffn_mm(ch, u[1])
                if u[1] > 0:
                    silu_chain(ch, u[1] - 1)
            elif u[0] == "down":
                down_itile(u[1])
            elif u[0] == "dtg":
                dt_group(ch, u[1])
            elif u[0] == "yfl":
                y_flush(ch, u[1], yadd, "try")
            else:
                scan_slice(ch, u[1], u[2], slice_idx, NG * NS)
                slice_idx += 1
        silu_chain(ch, IT - 1)
        pfree("g0", "g1", "u0", "u1", "dt", "dt1", "try")

    def ag_chain():
        nr = p.seq[V][1]
        for d in range(DT):
            p.wseq(S, V, nr)
            p.op(S, lambda e, d=d: e.activation(
                ccsb[:, DN + d * NS: DN + (d + 1) * NS],
                apos[:, d * NS:(d + 1) * NS], AF.Exp, scale=rsumb[:, d:d + 1]))
        nee = p.seq[S][1]
        ncp = p.op(V, lambda e: e.tensor_copy(ccsb[:, 0:DN], carry[:]))
        p.wseq(Y, V, ncp)
        p.wseq(Y, S, nee)
        p.dma(Y, lambda e: e.dma_start(out=cc_in, in_=ccsb[:]), ccw)
        ccs = p.sem("ccdone")
        p.wsem(Q, ccw, p.dtot(ccw))
        ccs[1] += 1
        p.ops[Q].append(lambda e, sm=ccs[0]: e.collective_compute(
            "AllGather", ALU.bypass, replica_groups=[list(range(NRANK))],
            ins=[cc_in], outs=[cc_out]).then_inc(sm))
        p.wait(Y, ccs, 1)
        prev = 0
        for r in range(NRANK):
            p.wseq(Y, V, prev)
            p.dma(Y, lambda e, r=r: e.dma_start(
                out=rkb[:], in_=cc_out[r * 128:(r + 1) * 128, :]), ccr)
            p.wsem(V, ccr, p.dtot(ccr))
            p.op(V, lambda e, r=r: e.tensor_scalar(
                ch1[:], rkb[:, DN:2 * DN], mske[:, r:r + 1], imske[:, r:r + 1],
                ALU.mult, ALU.add))
            p.op(V, lambda e: e.tensor_mul(ch1[:], h_in[:], ch1[:]))
            p.op(V, lambda e, r=r: e.tensor_scalar(
                ch2[:], rkb[:, 0:DN], mske[:, r:r + 1], None, ALU.mult))
            prev = p.op(V, lambda e: e.tensor_add(h_in[:], ch1[:], ch2[:]))
        last[("chain",)] = prev

    def corr_group(ch, g, slice_base, ygb=None):
        if ygb is None:
            ygb = yg
        ncd = 0
        for k in range(G):
            d = g * G + k
            par = k % 2
            p.wseq(S, V, last.get(("ndtr_used", par), 0))
            p.dma(S, lambda e, ch=ch, d=d, par=par: e.dma_start(
                out=dA2r[:, par * TC:(par + 1) * TC], in_=ndt_scr[ch, d]),
                ndtrd + str(par))
            p.wsem(V, ndtrd + str(par), p.dtot(ndtrd + str(par)))
            init = 0.0 if ch == 0 else cdtc[:, d:d + 1]
            p.wseq(V, S, last.get(("corr_s",), 0))
            ncs = p.op(V, lambda e, k=k, par=par, init=init: e.tensor_tensor_scan(
                sp1[:, k * TC:(k + 1) * TC], dA2r[:, par * TC:(par + 1) * TC],
                zeros_sb[:], init, ALU.add, ALU.add))
            last[("ndtr_used", par)] = ncs
            p.op(V, lambda e, k=k, d=d: e.tensor_copy(
                cdtc[:, d:d + 1], sp1[:, k * TC + TC - 1: k * TC + TC]))
            ncd = p.seq[V][1]
        for n in range(NS):
            slice_idx = slice_base + n
            par = slice_idx % 2
            cpar = slice_idx % RD
            rtarget = 16 * (slice_idx // RD + 1) + last[("crepl0", ch, cpar)]
            p.wsem(V, repl + "c" + str(cpar), rtarget)
            p.wsem(S, repl + "c" + str(cpar), rtarget)
            for k in range(G):
                d = g * G + k
                apar = (slice_idx * G + k) % 4
                p.wseq(S, V, ncd)
                p.wseq(S, V, last.get(("ec_used", apar), 0))
                nec = p.op(S, lambda e, k=k, d=d, n=n, apar=apar: e.activation(
                    bexp[:, apar * TC:(apar + 1) * TC], sp1[:, k * TC:(k + 1) * TC],
                    AF.Exp, scale=apos[:, d * NS + n: d * NS + n + 1]))
                last[("corr_s",)] = nec
                p.wseq(V, S, nec)
                p.wseq(V, V, last[("chain",)])
                ntm = p.op(V, lambda e, d=d, n=n, apar=apar, cpar=cpar:
                           e.scalar_tensor_tensor(
                    tmpb[:], bexp[:, apar * TC:(apar + 1) * TC],
                    h_in[:, d * NS + n: d * NS + n + 1],
                    cexp[:, cpar * TC:(cpar + 1) * TC], ALU.mult, ALU.mult))
                last[("ec_used", apar)] = ntm
                if n == 0:
                    p.wseq(V, T, last.get(("ytr_T",), 0))
                    p.op(V, lambda e, k=k, ygb=ygb: e.tensor_copy(
                        ygb[:, k * TC:(k + 1) * TC], tmpb[:]))
                else:
                    p.op(V, lambda e, k=k, ygb=ygb: e.tensor_add(
                        ygb[:, k * TC:(k + 1) * TC], ygb[:, k * TC:(k + 1) * TC],
                        tmpb[:]))
            nxt = slice_idx + RD - 1
            if nxt < NG * NS:
                npar = nxt % RD
                nn2 = nxt % NS
                p.wseq(S, V, last.get(("cslice_v", npar), 0))
                p.dma(S, lambda e, ch=ch, nn2=nn2, npar=npar: e.dma_start(
                    out=cexp[:, npar * TC:(npar + 1) * TC],
                    in_=bc_scr[ch, NS + nn2: NS + nn2 + 1,
                               :].broadcast_to([128, TC])), repl + "c" + str(npar))
            last[("cslice_v", cpar)] = p.seq[V][1]

    BW = 512

    def b_hc_mm(ch, hp):
        for s in range(TT):
            palloc(f"pd{s}", [128, BW], F32)
        nmm = None
        for j in range(IT):
            par = j % 4
            p.wait(Y, p.seq[T], last.get(("intR_used", par), 0))
            p.dma(Y, lambda e, ch=ch, j=j, par=par: e.dma_start(
                out=intR[:, par * TC:(par + 1) * TC], in_=int_scr[ch, j]),
                intr + str(par))
            p.wait(Q, p.seq[T], last.get(("intR_used", par), 0))
            p.dma(Q, lambda e, j=j, hp=hp, par=par: e.dma_start(
                out=dsl[:, par * BW:(par + 1) * BW],
                in_=wd_scr[j, :, hp * BW:(hp + 1) * BW]), dslr + str(par))
            p.wsem(T, intr + str(par), p.dtot(intr + str(par)))
            p.wsem(T, dslr + str(par), p.dtot(dslr + str(par)))
            if j == 0:
                for s in range(TT):
                    w = last.get(("pd_used", s))
                    if w:
                        p.wseq(T, V, w)
                p.wseq(T, V, last.get(("ytr_used",), 0))
            for s in range(TT):
                em = lambda e, par=par, s=s, j=j, ph=ps[f"pd{s}"]: e.matmul(
                        ph[:],
                        intR[:, par * TC + s * 128: par * TC + (s + 1) * 128],
                        dsl[:, par * BW:(par + 1) * BW],
                        start=(j == 0), stop=(j == IT - 1))
                if s == TT - 1:
                    nmm = p.op(T, em)
                else:
                    p.raw(T, em)
            last[("intR_used", par)] = p.seq[T][1]
        last[("b_nmm",)] = nmm

    def b_hc_evac(ch, hp):
        nmm = last.pop(("b_nmm",))
        for s in range(TT):
            dp = s % 2
            p.wseq(V, T, nmm)
            p.wsem(V, dadd + str(dp), last.get(("dstg_war", dp), 0))
            nev = p.op(V, lambda e, s=s, dp=dp, ph=ps[f"pd{s}"]: e.tensor_copy(
                dstg[:, dp * BW:(dp + 1) * BW], ph[:]))
            last[("pd_used", s)] = nev
            p.wseq(Q, V, nev)
            p.dma(Q, lambda e, ch=ch, hp=hp, s=s, dp=dp: e.dma_start(
                out=out[ch * TC + s * 128: ch * TC + (s + 1) * 128,
                        hp * BW:(hp + 1) * BW],
                in_=dstg[:, dp * BW:(dp + 1) * BW], accum_op=ALU.add),
                dadd + str(dp))
            last[("dstg_war", dp)] = p.dtot(dadd + str(dp))
        pfree(*[f"pd{s}" for s in range(TT)])

    def b_and_corr(ch):
        for s0 in range(RD):
            last[("crepl0", ch, s0)] = p.dtot(repl + "c" + str(s0))
        for s0 in range(RD - 1):
            nn0 = s0 % NS
            p.wseq(S, V, last.get(("cslice_v", s0), 0))
            p.dma(S, lambda e, ch=ch, nn0=nn0, s0=s0: e.dma_start(
                out=cexp[:, s0 * TC:(s0 + 1) * TC],
                in_=bc_scr[ch, NS + nn0: NS + nn0 + 1,
                           :].broadcast_to([128, TC])), repl + "c" + str(s0))
        HP = H // BW
        AM = cfg.get("AMODE", 9)
        gpw = NG // HP if NG % HP == 0 else None
        pending = []

        def do_flush(fg):
            last[("flush_gate", cadd, ch)] = (dadd + "0", p.dtot(dadd + "0"),
                                              dadd + "1", p.dtot(dadd + "1"))
            palloc("tryc", [128, 128], F32)
            for s in range(TT):
                w = last.get(("pd_used", s))
                if w:
                    p.wseq(T, V, w)
            y_flush(ch, fg, cadd, "tryc", ygb=(yg if fg % 2 == 0 else yg2))
            pfree("tryc")

        def corr_one(g):
            corr_group(ch, g, g * NS, ygb=(yg if g % 2 == 0 else yg2))
            pending.append(g)

        if AM >= 4 and gpw is not None and gpw <= 2:
            for u in range(HP):
                b_hc_mm(ch, u)
                for g in range((u - 1) * gpw, u * gpw):
                    if g >= 0:
                        corr_one(g)
                b_hc_evac(ch, u)
                for fg in pending:
                    do_flush(fg)
                del pending[:]
            for g in range((HP - 1) * gpw, NG):
                while pending and pending[0] <= g - 2:
                    do_flush(pending.pop(0))
                corr_one(g)
            for fg in pending:
                do_flush(fg)
            del pending[:]
        else:
            bi = 0
            for g in range(NG):
                need_hp = ((g + 1) * G * 128 + BW - 1) // BW
                while bi < min(need_hp, HP):
                    b_hc_mm(ch, bi)
                    b_hc_evac(ch, bi)
                    bi += 1
                    for fg in pending:
                        do_flush(fg)
                    del pending[:]
                if AM < 4:
                    continue
                while pending and pending[0] <= g - 2:
                    do_flush(pending.pop(0))
                corr_one(g)
            while bi < HP:
                b_hc_mm(ch, bi)
                b_hc_evac(ch, bi)
                bi += 1
            for fg in pending:
                do_flush(fg)
            del pending[:]

    PH = cfg.get("PHASES", 5)
    a_phase(0)
    p.barrier()
    if PH >= 2:
        a_phase(1)
        p.barrier()
    if PH >= 3:
        ag_chain()
    if PH >= 4:
        b_and_corr(0)
        p.barrier()
    if PH >= 5:
        b_and_corr(1)

    for nm, v in p.dsem.items():
        p.wait(Y, v, v[1])
    for e in Plan.ENGS:
        if e != Y:
            p.wait(Y, p.seq[e], p.seq[e][1])

    p.emit()
    return nc



def _prep_shared(A_log, Dv, x_proj_w, dt_proj_w, dt_proj_b,
                 gate_q, gate_s, up_q, up_s, down_q, down_s, cfg):
    HT, IT, NS = cfg["HT"], cfg["IT"], cfg["NS"]
    H = 128 * HT
    DT = HT
    c = np.ascontiguousarray

    def gu_blocks(q):
        return c(np.asarray(q, np.int32).T.reshape(HT, 128, IT, 128)
                 .transpose(2, 1, 0, 3).reshape(IT, 128, H))

    def gu_srows(s):
        return c(np.asarray(s, np.float32).reshape(IT, 128, HT)
                 .transpose(0, 2, 1).reshape(IT, H))

    import ml_dtypes
    return dict(
        gblk=gu_blocks(gate_q), ublk=gu_blocks(up_q),
        dblk=c(np.asarray(down_q, np.int32).T.reshape(IT, 128, H)),
        gsr=gu_srows(gate_s), usr=gu_srows(up_s),
        dsr=c(np.asarray(down_s, np.float32).T),
        xpw=c(np.asarray(x_proj_w, np.float32).T.reshape(HT, 128, 160)
              .transpose(1, 0, 2).reshape(128, HT * 160)),
        dtw=c(np.asarray(dt_proj_w, np.float32).T),
        alog=c(np.asarray(A_log, np.float32).reshape(DT, 128, NS)
               .transpose(1, 0, 2).reshape(128, DT * NS)),
        dvec=c(np.asarray(Dv, np.float32).reshape(DT, 128).T),
        dtbb=c(np.asarray(dt_proj_b, np.float32).reshape(DT, 128).T),
        eyeb=np.eye(128, dtype=ml_dtypes.bfloat16),
        eyef=np.eye(128, dtype=np.float32),
        ones_r=np.ones((1, 128), np.float32),
    )


def _mk_masks(cfg):
    NR, PB = cfg["NRANK"], cfg["PERB"]
    msks = []
    for j in range(NR):
        b, pos = j // PB, j % PB
        m = np.zeros((1, NR), np.float32)
        for r in range(NR):
            if r // PB == b and r % PB < pos:
                m[0, r] = 1.0
        msks.append(m)
    return msks


def kernel(L0_attn_x, L0_attn_A_log, L0_attn_D, L0_attn_x_proj_w,
           L0_attn_dt_proj_w, L0_attn_dt_proj_b, L0_ffn_gate_q, L0_ffn_gate_s,
           L0_ffn_up_q, L0_ffn_up_s, L0_ffn_down_q, L0_ffn_down_s):
    cfg = FULL
    NR = cfg["NRANK"]
    TOK = cfg["CH"] * cfg["TC"]
    H = 128 * cfg["HT"]
    x = np.ascontiguousarray(np.asarray(L0_attn_x, np.float32).reshape(-1, H))
    shared = _prep_shared(L0_attn_A_log, L0_attn_D, L0_attn_x_proj_w,
                          L0_attn_dt_proj_w, L0_attn_dt_proj_b, L0_ffn_gate_q,
                          L0_ffn_gate_s, L0_ffn_up_q, L0_ffn_up_s,
                          L0_ffn_down_q, L0_ffn_down_s, cfg)
    msks = _mk_masks(cfg)
    in_maps = []
    for c_ in range(NR):
        im = dict(shared)
        im["xs"] = np.ascontiguousarray(x[c_ * TOK:(c_ + 1) * TOK])
        im["msk"] = msks[c_]
        im["imsk"] = np.ascontiguousarray(1.0 - msks[c_])
        in_maps.append(im)
    nc = build_nc(cfg)
    res = run_bass_kernel_spmd(nc, in_maps, core_ids=list(range(NR)))
    outs = [np.asarray(r["out"], np.float32) for r in res.results]
    return np.concatenate(outs, 0).reshape(np.asarray(L0_attn_x).shape)
